# version 08 (final)
# baseline (speedup 1.0000x reference)
import os
import numpy as np
from contextlib import ExitStack

import concourse.bass as bass
import concourse.mybir as mybir
from concourse.bass_utils import run_bass_kernel_spmd

F32 = mybir.dt.float32
BF16 = mybir.dt.bfloat16
I32 = mybir.dt.int32
AF = mybir.ActivationFunctionType
ALU = mybir.AluOpType
AX = mybir.AxisListType.X

NCORE = 8
B = 8
PL = 256
T = 2048
D = 1024
H = 16
NT = 16
ND = 8
TWO_PI = float(2 * np.pi)
PI = float(np.pi)
LN_EPS = 1e-5
SBASE = 16512

ENGINES = ("gpsimd", "sync", "tensor", "scalar", "vector")



CHAINED_OPS = {
    "memset", "memzero", "tensor_copy", "tensor_tensor", "tensor_add",
    "tensor_sub", "tensor_mul", "tensor_max", "tensor_scalar",
    "tensor_single_scalar", "tensor_scalar_add", "tensor_scalar_sub",
    "tensor_scalar_mul", "tensor_scalar_max", "tensor_scalar_min",
    "tensor_reduce", "reduce_sum", "reduce_max", "reciprocal", "activation",
    "copy", "add", "mul", "square", "sqrt", "iota", "affine_select",
    "bn_stats", "bn_aggr", "select", "stream_shuffle", "scalar_tensor_tensor",
    "sign", "tensor_relu", "transpose", "partition_broadcast",
}


class Chain:

    def __init__(self, E, sems, semname):
        self._E = E
        self._sem = sems[semname]
        self._prev = None
        self._edge = None
        self._n = 0

    def _pre(self):
        if self._prev is not None:
            if self._edge is None:
                self._n += 1
                self._prev.then_inc(self._sem, 1)
                self._edge = (self._sem, self._n)
            self._E.wait_ge(self._edge[0], self._edge[1])

    def _emit(self, name, *a, **k):
        self._pre()
        inst = getattr(self._E, name)(*a, **k)
        self._prev = inst
        self._edge = None
        return inst

    def note_inc(self, inst, sem, val):
        if inst is self._prev:
            self._edge = (sem, val)

    def __getattr__(self, name):
        if name in CHAINED_OPS:
            return lambda *a, **k: self._emit(name, *a, **k)
        return getattr(self._E, name)


class PB:
    def __init__(self, nc, sems):
        self.nc = nc
        self.sems = sems
        self.cur = None
        self.E = None
        self.counts = {k: 0 for k in sems}
        self.marks = {}

    def begin(self, eng, obj):
        self.cur = eng
        if eng in ("scalar", "vector", "gpsimd"):
            self.E = Chain(obj, self.sems, "cc_" + eng)
        else:
            self.E = obj
        self.counts = {k: 0 for k in self.sems}

    def op(self, eng, fn, inc=None, iv=1, waits=()):
        if self.cur == eng:
            best = {}
            for w in waits:
                s, v = self.marks[w] if isinstance(w, str) else w
                if v > 0:
                    best[s] = max(best.get(s, 0), v)
            for s, v in best.items():
                self.E.wait_ge(self.sems[s], v)
            inst = fn(self.E)
            if inc is not None:
                assert inst is not None, f"op for {eng} returned None with inc={inc}"
                inst.then_inc(self.sems[inc], iv)
                if isinstance(self.E, Chain):
                    self.E.note_inc(inst, self.sems[inc], self.counts[inc] + iv)
        if inc is not None:
            self.counts[inc] += iv

    def mark(self, name, sem):
        self.marks[name] = (sem, self.counts[sem])

    def dma(self, eng, fn, waits=(), sem="ld"):
        self.op(eng, fn, inc=sem, iv=16, waits=waits)

    def wait(self, eng, waits):
        if self.cur == eng:
            best = {}
            for w in waits:
                s, v = self.marks[w] if isinstance(w, str) else w
                if v > 0:
                    best[s] = max(best.get(s, 0), v)
            for s, v in best.items():
                self.E.wait_ge(self.sems[s], v)


def build(debug=1, phases=99):
    nc = bass.Bass("TRN2", target_bir_lowering=False, detect_race_conditions=True)

    x_in = nc.dram_tensor("x", [B, PL, D], F32, kind="ExternalInput").ap()
    wqkv = nc.dram_tensor("wqkv", [D, 3 * D], F32, kind="ExternalInput").ap()
    wout_in = nc.dram_tensor("wout", [D, D], F32, kind="ExternalInput").ap()
    bout_in = nc.dram_tensor("bout", [1, D], F32, kind="ExternalInput").ap()
    wffn = nc.dram_tensor("wffn", [D, D], F32, kind="ExternalInput").ap()
    bvec_in = nc.dram_tensor("bvec", [4, D], F32, kind="ExternalInput").ap()
    inc_in = nc.dram_tensor("incv", [D], F32, kind="ExternalInput").ap()
    trow_in = nc.dram_tensor("trow", [1, T], F32, kind="ExternalInput").ap()

    out_t = nc.dram_tensor("out", [B, PL, D], F32, kind="ExternalOutput").ap()

    cin_k = nc.dram_tensor("cin_k", [B, ND, 128, PL], BF16).ap()
    cin_v = nc.dram_tensor("cin_v", [B, 2, 128, D], BF16).ap()
    cin_w = nc.dram_tensor("cin_w", [ND, 128, D], BF16).ap()
    cin_kn = nc.dram_tensor("cin_kn", [16, 8], F32).ap()
    agk = nc.dram_tensor("agk", [NCORE, B, ND, 128, PL], BF16, addr_space="Shared").ap()
    agv = nc.dram_tensor("agv", [B, NCORE, 2, 128, D], BF16, addr_space="Shared").ap()
    agw = nc.dram_tensor("agw", [NCORE, ND, 128, D], BF16, addr_space="Shared").ap()
    agkn = nc.dram_tensor("agkn", [NCORE, 16, 8], F32, addr_space="Shared").ap()
    crow_d = nc.dram_tensor("crow_d", [16, 8], F32).ap()
    scr_xres = nc.dram_tensor("scr_xres", [B, PL, D], F32).ap()

    dbg = {}

    def dbgt(name, shape, cond=True, dtype=F32):
        if debug and cond:
            dbg[name] = nc.dram_tensor("d_" + name, shape, dtype, kind="ExternalOutput").ap()
        return dbg.get(name)

    def AT(name, shape, dtype, kb):
        return nc.alloc_sbuf_tensor_at(name, shape, dtype, offset=SBASE + (int(kb * 1024) + 31) // 32 * 32)

    ident = AT("ident", [128, 128], BF16, 0)
    pt2 = AT("pt2", [128, 128], BF16, 0.25)
    bones = AT("bones", [128, 8 * 16], BF16, 0.5)
    frow = AT("frow", [1, 64], F32, 0.75)
    fcol = AT("fcol", [1, 128], F32, 1.0)
    onescol = AT("onescol", [128, 128], F32, 1.5)
    biasC = AT("biasC", [128, 128], F32, 2.0)
    st1m = AT("st1m", [128, 16], F32, 2.5)
    st1r = AT("st1r", [128, 16], F32, 2.5 + 1 / 16)
    st1a = AT("st1a", [128, 16], F32, 2.5 + 2 / 16)
    st1b = AT("st1b", [128, 16], F32, 2.5 + 3 / 16)
    st2m = AT("st2m", [128, 16], F32, 2.5 + 4 / 16)
    st2r = AT("st2r", [128, 16], F32, 2.5 + 5 / 16)
    sc1 = AT("sc1", [128, 16], F32, 2.5 + 6 / 16)
    sc2 = AT("sc2", [128, 16], F32, 2.5 + 7 / 16)
    sa0 = AT("sa0", [128, 16], F32, 2.5 + 8 / 16)
    sa1 = AT("sa1", [128, 16], F32, 2.5 + 9 / 16)
    sb0 = AT("sb0", [128, 16], F32, 2.5 + 10 / 16)
    sb1 = AT("sb1", [128, 16], F32, 2.5 + 11 / 16)
    knm = AT("knm", [16, 8], F32, 3.25)
    qnm = AT("qnm", [16, 8], F32, 3.25 + 1 / 16)
    cmb = AT("cmb", [16, 8], F32, 3.25 + 2 / 16)
    crow = AT("crow", [1, 128], F32, 3.25 + 4 / 16)
    kn8 = AT("kn8", [16, 64], F32, 3.75)
    bcol = AT("bcol", [128, 32], F32, 4.0)
    inccol = AT("inccol", [128, 8], F32, 4.15)
    vspc = AT("vspc", [1, 64], F32, 4.2)
    sspc = AT("sspc", [1, 64], F32, 4.2 + 8 / 1024)
    otn128 = AT("otn128", [128, ND * PL], BF16, 6)
    crow2 = AT("crow2", [1, 64], F32, 5.75)
    c_bf0 = AT("c_bf0", [128, ND * PL], BF16, 6)
    sin2 = AT("sin2", [128, T], F32, 10)
    cos2 = AT("cos2", [128, T], F32, 18)
    w_out_sb = AT("w_out_sb", [128, ND * D], BF16, 10)
    cuo_bo = AT("cuo_bo", [128, 2 * D], F32, 10)
    bo_b = AT("bo_b", [128, D], F32, 18)
    borow2 = AT("borow2", [1, D], F32, 22)
    rbf = AT("rbf", [128, 2 * 512], BF16, 26)
    EbufM = AT("EbufM", [128, 4 * 1024], BF16, 26)
    xnst0 = AT("xnst0", [128, D], BF16, 26)
    xnst1 = AT("xnst1", [128, D], BF16, 28)
    angA = AT("angA", [128, 512], F32, 29)
    angB = AT("angB", [128, 512], F32, 31)
    angC = AT("angC", [128, 512], F32, 33)
    angI = AT("angI", [128, 512], I32, 35)
    rr1 = AT("rr1", [128, 2 * 512], F32, 29)
    rr2 = AT("rr2", [128, 2 * 512], F32, 33)
    g1s = AT("g1s", [128, PL], F32, 29)
    g2s = AT("g2s", [128, PL], F32, 30)
    preslot = AT("preslot", [128, 2 * 512], F32, 29)
    tanslot = AT("tanslot", [128, 2 * 512], F32, 33)
    bo2 = AT("bo2", [128, 640], BF16, 4.5)
    nrm = AT("nrm", [128, 512], F32, 34)
    otnstage = AT("otnstage", [128, 512], BF16, 36)
    sq0 = AT("sq0", [128, D], F32, 37)
    sq1 = AT("sq1", [128, D], F32, 41)
    nsq0 = AT("nsq0", [128, T], BF16, 37)
    nsq1 = AT("nsq1", [128, T], BF16, 41)
    xch = AT("xch", [128, 2 * 512], F32, 37)
    xrs = AT("xrs", [128, 2 * 512], F32, 41)
    xrl = AT("xrl", [128, 2 * D], F32, 37)
    qT = AT("qT", [128, ND * T], BF16, 45)
    xn2T = AT("xn2T", [128, ND * T], BF16, 45)
    wnat = AT("wnat", [128, ND * D], BF16, 45)
    wTown = AT("wTown", [128, ND * D], BF16, 61)
    xt = AT("xt", [128, NT * D], F32, 77)
    kv_k = [AT("kv_k0", [128, ND * T], BF16, 77),
            AT("kv_k1", [128, ND * T], BF16, 109)]
    kTown = AT("kTown", [128, ND * T], BF16, 109)
    trow_sb = AT("trow_sb", [1, T], F32, 141)
    xnT = AT("xnT", [128, ND * T], BF16, 141)
    vown = AT("vown", [128, NT * D], BF16, 77)
    kv_v = [AT("kv_v0", [128, 16 * 1040], BF16, 141),
            AT("kv_v1", [128, 16 * 1040], BF16, 174)]
    wstrip = AT("wstrip", [128, 2 * ND * 512], BF16, 174)
    Wg = [AT(f"Wg{i}", [128, ND * D], BF16, 77 + 16 * i) for i in range(6)]
    uoT = AT("uoT", [128, ND * D], BF16, 77)
    woT = AT("woT", [128, ND * D], BF16, 93)
    c_st = [AT("c0", [128, ND * PL], F32, 173), AT("c1", [128, ND * PL], F32, 181)]
    c_bf1 = AT("c_bf1", [128, ND * PL], BF16, 189)
    ftb = AT("ftb", [128, PL], F32, 193)
    itb = AT("itb", [128, PL], F32, 194)
    gtb = AT("gtb", [128, PL], F32, 195)
    ones2 = AT("ones2", [128, 256], F32, 196)

    es = ExitStack()
    pA = es.enter_context(nc.psum_tensor("pA", [128, 1024], F32))
    pB = es.enter_context(nc.psum_tensor("pB", [128, 1024], F32))
    pC = es.enter_context(nc.psum_tensor("pC", [128, 512], F32))
    pD = es.enter_context(nc.psum_tensor("pD", [128, 512], F32))
    pE = es.enter_context(nc.psum_tensor("pE", [128, 512], F32))
    pF = es.enter_context(nc.psum_tensor("pF", [128, 512], F32))

    semnames = ["lda", "ldx", "lw0", "lw1", "lwf", "lwo", "lwg", "ldk0", "ldk1", "ldv0", "ldv1",
    "ldc0", "ldc1", "ldm0", "ldm1", "ldr0", "ldr1", "ldb", "ldn", "cdk", "cdv", "cdc", "cdx0", "cdx1",
    "cg", "ct", "cs", "cv", "cc", "sto", "stg", "dg0", "dg1", "so0", "so1", "cc_scalar", "cc_vector", "cc_gpsimd"]
    sems = {n: es.enter_context(nc.semaphore(n)) for n in semnames}
    pb = PB(nc, sems)

    strips = [("k", D), ("k", D + 512), ("v", 2 * D), ("v", 2 * D + 512),
              ("q", 0), ("q", 512)]

    d_xnT = dbgt("xnT", [128, ND * T], phases >= 1)
    d_st = dbgt("st", [128, 64], True)
    d_xt = dbgt("xt", [128, 2 * D], True)
    d_qT = dbgt("qT", [128, ND * T], phases >= 2)
    d_kT = dbgt("kT", [128, ND * T], phases >= 2)
    d_v = dbgt("v", [128, NT * D], phases >= 2)
    d_C = dbgt("C", [128, 128], phases >= 3)
    d_otn = dbgt("otn", [B, 128, ND * PL], phases >= 4)
    d_xres = dbgt("xres", [B, PL, D], phases >= 4)
    d_xn2T = dbgt("xn2T", [128, ND * T], phases >= 5)
    d_cf = dbgt("cf", [128, ND * PL], phases >= 6)
    d_g00 = dbgt("g00", [3, 128, PL], phases >= 6)
    d_c1 = dbgt("c1", [128, ND * PL], phases >= 6)
    d_pre0 = dbgt("pre0", [3, 128, PL], phases >= 6)
    d_wg0 = dbgt("wg0", [128, ND * D], phases >= 6, dtype=BF16)

    def xnT_dst(base, i):
        return base[:, 0:ND * T].rearrange("p (a c) -> p a c", a=ND)[:, :, i * 128:(i + 1) * 128]

    NG = 8
    GRP = 2

    def prog(pb):
        def g_c1(E):
            E.memset(ident[:, :], 0.0)
            E.memset(pt2[:, :], 0.0)
            E.memset(bones[:, :], 0.0)
            return E.memset(onescol[:, :], 0.0)
        pb.op("gpsimd", g_c1, inc="cg")
        pb.mark("g_c1", "cg")

        def g_c2(E):
            E.affine_select(out=ident[:, :], in_=ident[:, :], compare_op=ALU.not_equal,
                            fill=1.0, base=0, pattern=[[-1, 128]], channel_multiplier=1)
            for q in range(2):
                sub = pt2[64 * q:64 * q + 64, 64 * q:64 * q + 64]
                E.affine_select(out=sub, in_=sub, compare_op=ALU.not_equal,
                                fill=-1.0, base=-1, pattern=[[-2, 64]], channel_multiplier=1)
            for dt in range(8):
                E.memset(bones[0:64, dt * 16 + 2 * dt:dt * 16 + 2 * dt + 1], 1.0)
                E.memset(bones[64:128, dt * 16 + 2 * dt + 1:dt * 16 + 2 * dt + 2], 1.0)
            E.memset(onescol[64:65, :], 1.0)
            E.memset(onescol[0:1, :], 1.0)
            return E.iota(frow[0:1, 0:32], pattern=[[1, 32]], base=0,
                          channel_multiplier=0, allow_small_or_imprecise_dtypes=True)
        pb.op("gpsimd", g_c2, inc="cg", waits=["g_c1"])
        pb.mark("g_frow", "cg")

        def g_c3(E):
            r = None
            for q in range(2):
                sub = pt2[64 * q:64 * q + 64, 64 * q:64 * q + 64]
                r = E.affine_select(out=sub, in_=sub, compare_op=ALU.not_equal,
                                    fill=1.0, base=64, pattern=[[-2, 64]], channel_multiplier=1)
            return r
        pb.op("gpsimd", g_c3, inc="cg", waits=["g_frow"])
        pb.mark("g_consts", "cg")

        pb.dma("gpsimd", lambda E: E.dma_start(
            out=wnat[:, :].rearrange("p (a c) -> p a c", a=ND),
            in_=wffn.rearrange("(a p) c -> p a c", p=128)), sem="lwf")
        pb.mark("lw_ffn", "lwf")

        pb.dma("sync", lambda E: E.dma_start(out=trow_sb[:, :], in_=trow_in[:, :]), sem="lda")
        pb.mark("lda_all", "lda")
        pb.dma("sync", lambda E: E.dma_start(
            out=xt[:, :].rearrange("p (b t c) -> p b t c", b=B, t=2),
            in_=x_in.rearrange("b (t p) d -> p b t d", t=2)), sem="ldx")
        pb.mark("ldx_all", "ldx")

        pb.op("scalar", lambda E: E.activation(frow[0:1, 0:32], frow[0:1, 0:32], AF.Exp,
                                               scale=-float(np.log(10000.0)) / 32.0),
              inc="cs", waits=["g_consts"])
        pb.mark("s_frow", "cs")

        def v_fcol(E):
            E.tensor_copy(fcol[0:1, 0:64:2], frow[0:1, 0:32])
            E.tensor_copy(fcol[0:1, 1:64:2], frow[0:1, 0:32])
            E.tensor_copy(fcol[0:1, 64:128:2], frow[0:1, 0:32])
            return E.tensor_copy(fcol[0:1, 65:128:2], frow[0:1, 0:32])
        pb.op("vector", v_fcol, inc="cv", waits=["s_frow"])
        pb.mark("v_fcol", "cv")

        TBC = 256
        for c in range(8 if phases >= 1 else 0):
            ps = (pC if c % 2 == 0 else pD)[:, 0:TBC]
            w = ["v_fcol", "lda_all"] if c == 0 else ([f"v_tab{c - 2}"] if c >= 2 else [])
            pb.op("tensor", lambda E, ps=ps, c=c: E.matmul(
                ps, fcol[0:1, :], trow_sb[0:1, c * TBC:(c + 1) * TBC],
                start=True, stop=True), inc="ct", waits=w)
            pb.mark(f"t_tab{c}", "ct")

            def v_tab(E, ps=ps, c=c):
                sl = slice(c * TBC, (c + 1) * TBC)
                aS, aC = angA[:, 0:TBC], angA[:, TBC:2 * TBC]
                bS, bC = angB[:, 0:TBC], angB[:, TBC:2 * TBC]
                iS, iC = angC[:, 0:TBC], angC[:, TBC:2 * TBC]
                jS, jC = angI[:, 0:TBC], angI[:, TBC:2 * TBC]
                tS, tC = sin2[:, sl], cos2[:, sl]
                E.tensor_scalar(aS, ps, 0.0, None, ALU.add)
                E.tensor_scalar(aC, ps, PI / 2, None, ALU.add)
                E.tensor_scalar(bS, aS, 1.0 / TWO_PI, None, ALU.mult)
                E.tensor_scalar(bC, aC, 1.0 / TWO_PI, None, ALU.mult)
                E.tensor_copy(jS, bS)
                E.tensor_copy(jC, bC)
                E.tensor_copy(bS, jS)
                E.tensor_copy(bC, jC)
                E.tensor_scalar(tS, bS, TWO_PI, None, ALU.mult)
                E.tensor_scalar(tC, bC, TWO_PI, None, ALU.mult)
                E.tensor_sub(iS, aS, tS)
                E.tensor_sub(iC, aC, tC)
                E.tensor_scalar(bS, iS, PI, None, ALU.is_gt)
                E.tensor_scalar(bC, iC, PI, None, ALU.is_gt)
                E.tensor_scalar(tS, bS, TWO_PI, None, ALU.mult)
                E.tensor_scalar(tC, bC, TWO_PI, None, ALU.mult)
                E.tensor_sub(aS, iS, tS)
                E.tensor_sub(aC, iC, tC)
                E.tensor_scalar(bS, aS, -PI, None, ALU.is_lt)
                E.tensor_scalar(bC, aC, -PI, None, ALU.is_lt)
                E.tensor_scalar(tS, bS, TWO_PI, None, ALU.mult)
                E.tensor_scalar(tC, bC, TWO_PI, None, ALU.mult)
                E.tensor_add(iS, aS, tS)
                return E.tensor_add(iC, aC, tC)
            w = [f"t_tab{c}"]
            if c >= 1:
                w.append(f"s_tab{c - 1}")
            pb.op("vector", v_tab, inc="cv", waits=w)
            pb.mark(f"v_tab{c}", "cv")

            def s_tab(E, c=c):
                sl = slice(c * TBC, (c + 1) * TBC)
                E.activation(sin2[:, sl], angC[:, 0:TBC], AF.Sin)
                return E.activation(cos2[:, sl], angC[:, TBC:2 * TBC], AF.Sin)
            pb.op("scalar", s_tab, inc="cs", waits=[f"v_tab{c}"])
            pb.mark(f"s_tab{c}", "cs")
        if phases >= 1:
            pb.marks["s_tab7"] = pb.marks["s_tab7"]

        for i in range(NT):
            sq = sq0 if i % 2 == 0 else sq1
            w = ["ldx_all"]
            if i >= 2:
                w.append(f"v_st{i - 2}")
            pb.op("scalar", lambda E, i=i, sq=sq: E.activation(
                sq[:, :], xt[:, i * D:(i + 1) * D], AF.Square), inc="cs", waits=w)
            pb.mark(f"s_sq{i}", "cs")

            def v_st(E, i=i, sq=sq):
                E.reduce_sum(st1a[:, i:i + 1], xt[:, i * D:(i + 1) * D], axis=AX)
                return E.reduce_sum(st1b[:, i:i + 1], sq[:, :], axis=AX)
            pb.op("vector", v_st, inc="cv", waits=[f"s_sq{i}"])
            pb.mark(f"v_st{i}", "cv")

        def v_fin1(E):
            E.tensor_scalar(sc1[:, :], st1a[:, :], 1.0 / D, None, ALU.mult)
            E.tensor_scalar(sa0[:, :], st1b[:, :], 1.0 / D, None, ALU.mult)
            E.tensor_mul(sc2[:, :], sc1[:, :], sc1[:, :])
            E.tensor_copy(st1m[:, :], sc1[:, :])
            E.tensor_sub(sa1[:, :], sa0[:, :], sc2[:, :])
            E.memset(vspc[:, :], 0.0)
            return E.tensor_scalar(sb0[:, :], sa1[:, :], LN_EPS, None, ALU.add)
        pb.op("vector", v_fin1, inc="cv", waits=[f"v_st{NT - 1}"])
        pb.mark("v_fin1", "cv")

        def s_rstd1(E):
            E.activation(sb1[:, :], sb0[:, :], AF.Ln)
            E.memzero(sspc[:, :])
            return E.activation(st1r[:, :], sb1[:, :], AF.Exp, scale=-0.5)
        pb.op("scalar", s_rstd1, inc="cs", waits=["v_fin1"])
        pb.mark("s_rstd1", "cs")

        for i in range(NT):
            xn = xnst0 if i % 2 == 0 else xnst1
            w = ["s_rstd1"]
            if i >= 2:
                w.append(f"t_tr{i - 2}")
            pb.op("vector", lambda E, i=i, xn=xn: E.tensor_scalar(
                xn[:, :], xt[:, i * D:(i + 1) * D], st1m[:, i:i + 1], st1r[:, i:i + 1],
                ALU.subtract, ALU.mult), inc="cv", waits=w)
            pb.mark(f"v_ap{i}", "cv")

            if phases < 1:
                pb.mark(f"t_tr{i}", "ct")
                pb.mark(f"s_te{i}", "cs")
                continue
            pt = pA if i % 2 == 0 else pB
            w = [f"v_ap{i}", "g_consts"]
            if i >= 2:
                w.append(f"s_te{i - 2}")
            def t_tr(E, xn=xn, pt=pt):
                r = None
                for dt in range(ND):
                    r = E.matmul(pt[:, dt * 128:(dt + 1) * 128],
                                 xn[:, dt * 128:(dt + 1) * 128], ident[:, :],
                                 start=True, stop=True)
                return r
            pb.op("tensor", t_tr, inc="ct", waits=w)
            pb.mark(f"t_tr{i}", "ct")

            wte = [f"t_tr{i}"] + (["t_tab7"] if (i == 0 and phases >= 1) else [])
            pb.op("scalar", lambda E, i=i, pt=pt: E.copy(
                xnT_dst(xnT, i), pt[:, :].rearrange("p (a c) -> p a c", a=ND)),
                inc="cs", waits=wte)
            pb.mark(f"s_te{i}", "cs")

        if d_xt is not None:
            pb.dma("sync", lambda E: E.dma_start(out=d_xt[:, 0:D], in_=xt[:, 0:D]),
                   waits=["ldx_all"], sem="sto")
            pb.dma("sync", lambda E: E.dma_start(out=d_xt[:, D:2 * D], in_=xt[:, D:2 * D]),
                   waits=["ldx_all"], sem="sto")

        if d_st is not None:
            for j, tsr in enumerate([st1m, st1r, st1a, st1b]):
                pb.dma("sync", lambda E, j=j, tsr=tsr: E.dma_start(
                    out=d_st[:, 16 * j:16 * (j + 1)], in_=tsr[:, :]),
                    waits=[f"s_te{NT - 1}"], sem="sto")
        pb.mark("dbg_xt", "sto")
        if d_xnT is not None and phases >= 1:
            pb.dma("gpsimd", lambda E: E.dma_start(out=d_xnT, in_=xnT[:, :]),
                   waits=[f"s_te{NT - 1}"], sem="stg")
        if phases < 2:
            pb.wait("gpsimd", [("sto", pb.counts["sto"]), ("stg", pb.counts["stg"]), ("cs", pb.counts["cs"]), ("cv", pb.counts["cv"])])
            return

        for g in range(ND):
            pt = pA if g % 2 == 0 else pB
            w = ["lw_ffn", f"s_te{NT - 1}"]
            if g >= 2:
                w.append(f"s_wt{g - 2}")
            def t_wt(E, g=g, pt=pt):
                r = None
                for fa in range(ND):
                    r = E.matmul(pt[:, fa * 128:(fa + 1) * 128],
                                 wnat[:, fa * D + g * 128: fa * D + g * 128 + 128],
                                 ident[:, :], start=True, stop=True)
                return r
            pb.op("tensor", t_wt, inc="ct", waits=w)
            pb.mark(f"t_wt{g}", "ct")
            pb.op("scalar", lambda E, g=g, pt=pt: E.copy(
                wTown[:, g * D:(g + 1) * D], pt[:, :]), inc="cs", waits=[f"t_wt{g}"])
            pb.mark(f"s_wt{g}", "cs")

        pb.dma("sync", lambda E: E.dma_start(
            out=cin_w.rearrange("a p c -> p a c"),
            in_=wTown[:, :].rearrange("p (a c) -> p a c", a=ND)),
            waits=[f"s_wt{ND - 1}"], sem="cdk")
        pb.mark("d_wT", "cdk")

        def strip_dma(s):
            kind, off = strips[s]
            buf = s % 2
            w = [f"t_strip{s - 2}"] if s >= 2 else []
            pb.dma("gpsimd", lambda E, off=off, buf=buf: E.dma_start(
                out=wstrip[:, buf * ND * 512:(buf + 1) * ND * 512]
                .rearrange("p (a c) -> p a c", a=ND),
                in_=wqkv[:, off:off + 512].rearrange("(a p) c -> p a c", p=128)),
                waits=w, sem="lw0" if s % 2 == 0 else "lw1")
            pb.mark(f"lw_strip{s}", "lw0" if s % 2 == 0 else "lw1")

        strip_dma(0)
        strip_dma(1)

        def norms(tag, srcT, outm, wait0):
            for dt in range(ND):
                nq = nsq0 if dt % 2 == 0 else nsq1
                w = list(wait0) if dt == 0 else ([f"t_ns{tag}{dt - 2}"] if dt >= 2 else [])
                pb.op("vector", lambda E, nq=nq, dt=dt, srcT=srcT: E.tensor_mul(
                    nq[:, :], srcT[:, dt * T:(dt + 1) * T], srcT[:, dt * T:(dt + 1) * T]),
                    inc="cv", waits=w)
                pb.mark(f"v_ns{tag}{dt}", "cv")
                def t_ns(E, nq=nq, dt=dt):
                    r = None
                    for tc in range(4):
                        ph = (pA if tc < 2 else pB)[0:16, (tc % 2) * 512:(tc % 2) * 512 + 512]
                        r = E.matmul(ph, bones[:, dt * 16:(dt + 1) * 16],
                                     nq[:, tc * 512:(tc + 1) * 512],
                                     start=(dt == 0), stop=(dt == ND - 1))
                    return r
                pb.op("tensor", t_ns, inc="ct", waits=[f"v_ns{tag}{dt}"])
                pb.mark(f"t_ns{tag}{dt}", "ct")
            def v_nred(E, outm=outm):
                r = None
                for tc in range(4):
                    ph = (pA if tc < 2 else pB)[0:16, (tc % 2) * 512:(tc % 2) * 512 + 512]
                    r = E.tensor_reduce(outm[:, tc * 2:tc * 2 + 2],
                                        ph.rearrange("p (a c) -> p a c", a=2),
                                        axis=AX, op=ALU.max)
                return r
            pb.op("vector", v_nred, inc="cv", waits=[f"t_ns{tag}{ND - 1}"])
            pb.mark(f"v_nred{tag}", "cv")

        ci = 0
        vi = 0
        for s, (kind, off) in enumerate(strips):
            buf = s % 2
            if kind in ("k", "q"):
                for fs in range(4):
                    for tc in range(4):
                        pah = pA[:, (ci % 2) * 512:(ci % 2) * 512 + 512]
                        pbh = pB[:, (ci % 2) * 512:(ci % 2) * 512 + 512]
                        w = [f"lw_strip{s}", f"s_te{NT - 1}"]
                        if ci <= 1:
                            w.append(f"s_wt{ND - 1}")
                        if kind == "q" and fs == 0 and tc == 0:
                            w.append("cdk_all")
                        if ci in (32, 33):
                            w.append("v_nredk")
                        if ci >= 2:
                            w.append(f"v_rope{ci - 2}")
                        def t_mm(E, pah=pah, fs=fs, tc=tc, buf=buf):
                            r = None
                            for dt in range(ND):
                                base = buf * ND * 512 + dt * 512 + fs * 128
                                r = E.matmul(pah, wstrip[:, base:base + 128],
                                             xnT[:, dt * T + tc * 512: dt * T + tc * 512 + 512],
                                             start=(dt == 0), stop=(dt == ND - 1))
                            return r
                        pb.op("tensor", t_mm, inc="ct", waits=w)
                        pb.mark(f"t_qkv{ci}", "ct")
                        rb = rbf[:, (ci % 2) * 512:(ci % 2) * 512 + 512]
                        pb.op("scalar", lambda E, rb=rb, pah=pah: E.copy(rb, pah),
                              inc="cs", waits=[f"t_qkv{ci}"])
                        pb.mark(f"s_kbf{ci}", "cs")
                        w2 = [f"s_kbf{ci}"]
                        if ci in (32, 33):
                            w2.append("v_nredk")
                        if ci >= 2:
                            w2.append(f"v_rope{ci - 2}")
                        pb.op("tensor", lambda E, pbh=pbh, rb=rb: E.matmul(
                            pbh, pt2[:, :], rb, start=True, stop=True),
                            inc="ct", waits=w2)
                        pb.mark(f"t_rot{ci}", "ct")
                        dstb = kTown if kind == "k" else qT
                        dtk = (off - (D if kind == "k" else 0)) // 128 + fs
                        dst = dstb[:, dtk * T + tc * 512: dtk * T + tc * 512 + 512]
                        def v_rope(E, pah=pah, pbh=pbh, dst=dst, tc=tc, ci=ci):
                            t1 = rr1[:, (ci % 2) * 512:(ci % 2) * 512 + 512]
                            t2 = rr2[:, (ci % 2) * 512:(ci % 2) * 512 + 512]
                            E.tensor_mul(t1, pah, cos2[:, tc * 512:(tc + 1) * 512])
                            E.tensor_mul(t2, pbh, sin2[:, tc * 512:(tc + 1) * 512])
                            E.memset(vspc[:, :], 0.0)
                            return E.tensor_add(dst, t1, t2)
                        pb.op("vector", v_rope, inc="cv", waits=[f"t_rot{ci}", "s_tab7"] + (["dbg_xt"] if ci == 0 and debug else []))
                        pb.mark(f"v_rope{ci}", "cv")
                        ci += 1
            else:
                half = (off - 2 * D) // 512
                for tt in range(NT):
                    ps = pC if vi % 2 == 0 else pD
                    w = [f"lw_strip{s}", f"s_te{NT - 1}", "s_tab7"]
                    if vi >= 2:
                        w.append(f"s_ve{vi - 2}")
                    def t_vm(E, ps=ps, tt=tt, buf=buf):
                        r = None
                        for dt in range(ND):
                            r = E.matmul(ps[:, :],
                                         xnT[:, dt * T + tt * 128: dt * T + tt * 128 + 128],
                                         wstrip[:, buf * ND * 512 + dt * 512:
                                                buf * ND * 512 + dt * 512 + 512],
                                         start=(dt == 0), stop=(dt == ND - 1))
                        return r
                    pb.op("tensor", t_vm, inc="ct", waits=w)
                    pb.mark(f"t_v{vi}", "ct")
                    pb.op("scalar", lambda E, ps=ps, tt=tt, half=half: E.copy(
                        vown[:, tt * D + half * 512: tt * D + half * 512 + 512], ps[:, :]),
                        inc="cs", waits=[f"t_v{vi}"] + (["dbg_xt"] if vi == 0 and debug else []))
                    pb.mark(f"s_ve{vi}", "cs")
                    vi += 1
            pb.mark(f"t_strip{s}", "ct")
            if s + 2 < len(strips):
                strip_dma(s + 2)

            if s == 1:
                for b_ in range(B):
                    pb.dma("sync", lambda E, b_=b_: E.dma_start(
                        out=cin_k[b_].rearrange("a p c -> p a c"),
                        in_=kTown[:, :].rearrange("p (a c) -> p a c", a=ND)
                        [:, :, b_ * PL:(b_ + 1) * PL]),
                        waits=([f"v_rope{31}"] if b_ == 0 else []), sem="cdk")
                    pb.mark(f"d_k{b_}", "cdk")
                norms("k", kTown, knm, [f"v_rope{31}"])
                pb.dma("sync", lambda E: E.dma_start(out=cin_kn[:, :], in_=knm[:, :]),
                       waits=["v_nredk"], sem="cdk")
                pb.mark("d_kn", "cdk")
                pb.mark("cdk_all", "cdk")
                pb.op("gpsimd", lambda E: E.collective_compute(
                    "AllGather", ALU.bypass, replica_groups=[list(range(NCORE))],
                    ins=[cin_k[:, :, :, :]], outs=[agk[:, :, :, :, :]]),
                    inc="cc", waits=["d_wT", "d_kn"] + [f"d_k{b_}" for b_ in range(B)])
                pb.op("gpsimd", lambda E: E.collective_compute(
                    "AllGather", ALU.bypass, replica_groups=[list(range(NCORE))],
                    ins=[cin_kn[:, :]], outs=[agkn[:, :, :]]), inc="cc")
                pb.mark("ag1", "cc")
            if s == 3:
                for b_ in range(B):
                    pb.dma("sync", lambda E, b_=b_: E.dma_start(
                        out=cin_v[b_].rearrange("j p c -> p j c"),
                        in_=vown[:, 2 * b_ * D:(2 * b_ + 2) * D]
                        .rearrange("p (a c) -> p a c", a=2)),
                        waits=([f"s_ve{31}"] if b_ == 0 else []), sem="cdv")
                    pb.mark(f"d_v{b_}", "cdv")
                pb.mark("d_v_all", "cdv")
                for b_ in range(B):
                    pb.op("gpsimd", lambda E, b_=b_: E.collective_compute(
                        "AllGather", ALU.bypass, replica_groups=[list(range(NCORE))],
                        ins=[cin_v[b_]], outs=[agv[b_]]),
                        inc="cc", waits=(["d_v_all"] if b_ == 0 else []))
                    pb.mark(f"ag2_{b_}", "cc")
                pb.op("gpsimd", lambda E: E.collective_compute(
                    "AllGather", ALU.bypass, replica_groups=[list(range(NCORE))],
                    ins=[cin_w[:, :, :]], outs=[agw[:, :, :, :]]), inc="cc")
                pb.mark("agw_done", "cc")

        norms("q", qT, qnm, [f"v_rope{63}"])

        if d_kT is not None:
            pb.dma("gpsimd", lambda E: E.dma_start(out=d_kT, in_=kTown[:, :]),
                   waits=[f"v_rope{31}"], sem="stg")
        if d_qT is not None:
            pb.dma("gpsimd", lambda E: E.dma_start(out=d_qT, in_=qT[:, :]),
                   waits=[f"v_rope{63}"], sem="stg")
        if d_v is not None:
            pb.dma("gpsimd", lambda E: E.dma_start(out=d_v, in_=vown[:, :]),
                   waits=[f"s_ve{31}"], sem="stg")
        pb.mark("dbg_xnT", "stg")
        if phases < 3:
            pb.wait("gpsimd", [("sto", pb.counts["sto"]), ("stg", pb.counts["stg"]),
                               ("cs", pb.counts["cs"]),
                               ("cv", pb.counts["cv"]), ("cc", pb.counts["cc"]),
                               ("cdk", pb.counts["cdk"]), ("cdv", pb.counts["cdv"])])
            return

        pb.dma("sync", lambda E: E.dma_start(
            out=kn8[:, :].rearrange("p (b r) -> p b r", b=8),
            in_=agkn.transpose([1, 2, 0]), allow_slow_non_contiguous=True),
            waits=["ag1"], sem="ldn")
        pb.mark("ld_kn8", "ldn")

        def v_kmax(E):
            E.tensor_reduce(knm[:, :], kn8[:, :].rearrange("p (b r) -> p b r", b=8),
                            axis=AX, op=ALU.max)
            E.memset(vspc[:, :], 0.0)
            E.tensor_add(cmb[:, :], knm[:, :], qnm[:, :])
            E.memset(vspc[:, :], 1.0)
            return E.tensor_scalar(qnm[:, :], cmb[:, :], 1.02 / 16.0, 0.05,
                                   ALU.mult, ALU.add)
        pb.op("vector", v_kmax, inc="cv", waits=["ld_kn8", "v_nredq"])
        pb.mark("v_cmb", "cv")
        pb.dma("sync", lambda E: E.dma_start(out=crow_d[:, :], in_=qnm[:, :]),
               waits=["v_cmb"], sem="cdc")
        pb.mark("d_cmb", "cdc")
        pb.dma("sync", lambda E: E.dma_start(
            out=crow[0:1, :], in_=crow_d.rearrange("a b -> (a b)")), waits=["d_cmb"], sem="ldn")
        pb.mark("ld_crow", "ldn")
        def v_cpair(E):
            cv = crow[0:1, :].rearrange("a (hp t b) -> a hp t b", hp=8, t=2)
            return E.tensor_max(crow2[0:1, :].rearrange("a (hp b) -> a hp b", hp=8),
                                cv[:, :, 0, :], cv[:, :, 1, :])
        pb.op("vector", v_cpair, inc="cv", waits=["ld_crow"])
        pb.mark("v_cpair", "cv")
        pb.op("tensor", lambda E: E.matmul(pE[:, 0:64], onescol[0:1, :], crow2[0:1, :],
                                           start=True, stop=True),
              inc="ct", waits=["v_cpair", f"s_te{NT - 1}", f"s_wt{ND - 1}"])
        pb.mark("t_biasC", "ct")
        pb.op("scalar", lambda E: E.activation(biasC[:, 0:64], pE[:, 0:64], AF.Copy,
                                               scale=-1.0), inc="cs", waits=["t_biasC"])
        pb.mark("s_biasC", "cs")
        if d_C is not None:
            pb.dma("sync", lambda E: E.dma_start(out=d_C[:, 0:64], in_=biasC[:, 0:64]),
                   waits=["s_biasC"], sem="sto")
        if phases < 4:
            pb.wait("gpsimd", [("sto", pb.counts["sto"]), ("stg", pb.counts["stg"]),
                               ("cs", pb.counts["cs"]),
                               ("cv", pb.counts["cv"]), ("cc", pb.counts["cc"]),
                               ("cdk", pb.counts["cdk"]), ("cdv", pb.counts["cdv"])])
            return

        pb.dma("gpsimd", lambda E: E.dma_start(
            out=w_out_sb[:, :].rearrange("p (a c) -> p a c", a=ND),
            in_=wout_in.rearrange("(a p) c -> p a c", p=128)),
            waits=[f"v_rope{63}"], sem="lwo")
        pb.mark("lw_wout", "lwo")

        pb.dma("gpsimd", lambda E: E.dma_start(
            out=bo2[0:1, 128:640], in_=bout_in[0:1, 0:512]), sem="lwf")
        pb.dma("gpsimd", lambda E: E.dma_start(
            out=bo2[64:65, 0:512], in_=bout_in[0:1, 512:1024]), sem="lwf")
        pb.mark("lw_bo2", "lwf")

        def g_vones(E):
            E.memset(bo2[0:1, 0:128], 1.0)
            E.memset(bo2[64:65, 512:640], 1.0)
            r = None
            for bf in range(2):
                r = E.memset(kv_v[bf][:, :].rearrange("p (j e) -> p j e", e=65)
                             [:, :, 64:65], 1.0)
            return r
        pb.op("gpsimd", g_vones, inc="cg",
              waits=[f"t_strip{5}", "v_nredq", f"d_v{B - 1}", "dbg_xnT"])
        pb.mark("g_vones", "cg")

        def kv_load(b_):
            buf = b_ % 2
            w = ["ag1", "g_vones"]
            if b_ == 0:
                w += [f"v_ap{NT - 1}"]
            if b_ == 1:
                w += ["v_nredq"]
            if b_ >= 2:
                w += [f"t_avd{b_ - 2}"]
            for r in range(NCORE):
                pb.dma("sync", lambda E, b_=b_, r=r, buf=buf: E.dma_start(
                    out=kv_k[buf][:, :].rearrange("p (a g c) -> p a g c", a=ND, g=NCORE)
                    [:, :, r, :],
                    in_=agk[r, b_].rearrange("a p c -> p a c")),
                    waits=(w if r == 0 else ()), sem="ldk0" if b_ % 2 == 0 else "ldk1")
                pb.mark(f"ld_kk{b_}_{r}", "ldk0" if b_ % 2 == 0 else "ldk1")
            w2 = [f"ag2_{b_}"] + (w + ["t_strip5"] if b_ <= 1 else [])
            for r in range(NCORE):
                for j in range(2):
                    pb.dma("gpsimd", lambda E, b_=b_, r=r, buf=buf, j=j: E.dma_start(
                        out=kv_v[buf][:, (2 * r + j) * 1040:(2 * r + j + 1) * 1040]
                        .rearrange("p (h e) -> p h e", h=16)[:, :, 0:64],
                        in_=agv[b_, r, j].rearrange("p (h e) -> p h e", h=16)),
                        waits=(w2 if r == 0 and j == 0 else ()),
                        sem="ldv0" if b_ % 2 == 0 else "ldv1")
                pb.mark(f"ld_kv{b_}_{r}", "ldv0" if b_ % 2 == 0 else "ldv1")

        kv_load(0)
        kv_load(1)

        NP = H // 2
        NGG = B * NP * NG

        def pair_tail_a(hhp):
            b2, hp = hhp // NP, hhp % NP
            bank = pC if hhp % 2 == 0 else pD
            GL = hhp * NG + NG - 1
            w = [f"t_aB{GL}"]
            if hhp >= 1:
                w.append(f"t_rb{hhp - 1}")
            def v_lp(E, bank=bank):
                return E.reciprocal(nrm[64:65, 0:512], bank[64:65, 0:512])
            pb.op("vector", v_lp, inc="cv", waits=w)
            pb.mark(f"v_lp{hhp}", "cv")

        def rest_tail(hhp):
            b2, hp = hhp // NP, hhp % NP
            bank = pC if hhp % 2 == 0 else pD
            pRB = pE if hhp % 2 == 0 else pF
            w = [f"v_lp{hhp}"]
            if hhp >= 2:
                w.append(f"s_rb{hhp - 2}")
            if hhp == 0:
                w.append("s_biasC")
            if hp == 1 and b2 >= 1:
                w.append(f"v_ye{b2 - 1}_3")
            pb.op("tensor", lambda E, pRB=pRB: E.matmul(
                pRB[0:64, 0:512], onescol[64:65, 0:64], nrm[64:65, 0:512],
                start=True, stop=True), inc="ct", waits=w)
            pb.mark(f"t_rb{hhp}", "ct")

            w = [f"t_rb{hhp}"]
            if hhp >= 1:
                w.append(f"v_noB{hhp - 1}")
            pb.op("vector", lambda E, pRB=pRB: E.tensor_copy(
                nrm[0:64, 0:512], pRB[0:64, 0:512]), inc="cv", waits=w)
            pb.mark(f"s_rb{hhp}", "cv")

            w = [f"s_rb{hhp}"]
            if hp == 0 and b2 >= 1:
                w.append(f"t_y{b2 - 1}_3")
                if d_otn is not None:
                    w.append(f"dbg_otn{b2 - 1}")
            pb.op("vector", lambda E, bank=bank, hp=hp: E.tensor_mul(
                otn128[0:64, hp * 256:hp * 256 + 256], bank[0:64, 0:256],
                nrm[0:64, 0:256]), inc="cv", waits=w)
            pb.mark(f"v_noA{hhp}", "cv")

            w = [f"s_rb{hhp}"]
            if hhp >= 2:
                w.append(f"ld_mv{hhp - 2}")
            pb.op("vector", lambda E, bank=bank, hp=hp: E.tensor_mul(
                otnstage[0:64, (hp % 2) * 256:(hp % 2) * 256 + 256],
                bank[0:64, 256:512], nrm[0:64, 256:512]), inc="cv", waits=w)
            pb.mark(f"v_noB{hhp}", "cv")

            w = [f"v_noB{hhp}"]
            if hp == 0 and b2 >= 1:
                w.append(f"t_y{b2 - 1}_3")
                if d_otn is not None:
                    w.append(f"dbg_otn{b2 - 1}")
            pb.dma("sync", lambda E, hp=hp: E.dma_start(
                out=otn128[64:128, hp * 256:hp * 256 + 256],
                in_=otnstage[0:64, (hp % 2) * 256:(hp % 2) * 256 + 256]),
                waits=w, sem="ldm0" if hp % 2 == 0 else "ldm1")
            pb.mark(f"ld_mv{hhp}", "ldm0" if hp % 2 == 0 else "ldm1")

        def emit_av(GG):
            hhp, g = GG // NG, GG % NG
            b2, hp = hhp // NP, hhp % NP
            buf2 = b2 % 2
            bank = pC if hhp % 2 == 0 else pD
            for side, hx in (("A", 2 * hp), ("B", 2 * hp + 1)):
                coff = 0 if side == "A" else 256
                w = [f"s_e{GG}", f"ld_kv{b2}_{NCORE - 1}"]
                if g == 0 and side == "A":
                    if hhp >= 2:
                        w.append(f"v_noB{hhp - 2}")
                eoff2 = (GG % 4) * 1024 + (0 if side == "A" else 512)
                def t_a(E, bank=bank, coff=coff, g=g, hx=hx, buf2=buf2, eoff2=eoff2, side=side):
                    r = None
                    for jj in range(GRP):
                        jt = g * GRP + jj
                        r = E.matmul(
                            bank[0:65, coff:coff + 256],
                            kv_v[buf2][:, jt * 1040 + hx * 65: jt * 1040 + hx * 65 + 65],
                            EbufM[:, eoff2 + jj * 256:eoff2 + jj * 256 + 256],
                            start=(side == "A" and g == 0 and jj == 0),
                            stop=(side == "B" and g == NG - 1 and jj == GRP - 1))
                    return r
                pb.op("tensor", t_a, inc="ct", waits=w)
                pb.mark(f"t_a{side}{GG}", "ct")

        def batch_tail(b_):
            for tt in range(2):
                for oc in range(2):
                    yc = tt * 2 + oc
                    w = []
                    if b_ >= 1 or yc >= 2:
                        pv = (b_, yc - 2) if yc >= 2 else (b_ - 1, yc + 2)
                        w.append(f"v_sx{pv[0]}_{pv[1]}")
                    pb.dma("gpsimd", lambda E, b_=b_, tt=tt, oc=oc, yc=yc: E.dma_start(
                        out=xch[:, (yc % 2) * 512:(yc % 2) * 512 + 512],
                        in_=x_in[b_, tt * 128:(tt + 1) * 128, oc * 512:(oc + 1) * 512]),
                        waits=w, sem="ldc0" if yc % 2 == 0 else "ldc1")
                    pb.mark(f"ld_xc{b_}_{yc}", "ldc0" if yc % 2 == 0 else "ldc1")

                    w = [f"v_noA{b_ * NP + NP - 1}", f"ld_mv{b_ * NP + NP - 1}",
                         f"ld_mv{b_ * NP + NP - 2}", "lw_wout", "lw_bo2", "g_vones"]
                    if b_ * 4 + yc >= 1:
                        pv = (b_, yc - 1) if yc >= 1 else (b_ - 1, 3)
                        w.append(f"v_ye{pv[0]}_{pv[1]}")
                    def t_y(E, tt=tt, oc=oc):
                        for ct in range(ND):
                            E.matmul(pF[:, :],
                                     otn128[:, ct * 256 + tt * 128: ct * 256 + tt * 128 + 128],
                                     w_out_sb[:, ct * D + oc * 512: ct * D + oc * 512 + 512],
                                     start=(ct == 0), stop=False)
                        if oc == 0:
                            return E.matmul(pF[:, :], bo2[0:1, 0:128],
                                            bo2[0:1, 128:640], start=False, stop=True)
                        return E.matmul(pF[:, :], bo2[64:65, 512:640],
                                        bo2[64:65, 0:512], start=False, stop=True)
                    pb.op("tensor", t_y, inc="ct", waits=w)
                    pb.mark(f"t_y{b_}_{yc}", "ct")

                    def v_ye(E, yc=yc):
                        return E.tensor_add(xrs[:, (yc % 2) * 512:(yc % 2) * 512 + 512],
                                            pF[:, :], xch[:, (yc % 2) * 512:(yc % 2) * 512 + 512])
                    w = [f"t_y{b_}_{yc}", f"ld_xc{b_}_{yc}"]
                    if b_ * 4 + yc >= 2:
                        pv = (b_, yc - 2) if yc >= 2 else (b_ - 1, yc + 2)
                        w.append(f"d_xr{pv[0]}_{pv[1]}")
                        if d_xres is not None:
                            w.append(f"dbg_xr{pv[0]}_{pv[1]}")
                    pb.op("vector", v_ye, inc="cv", waits=w)
                    pb.mark(f"v_ye{b_}_{yc}", "cv")

                    i2 = b_ * 2 + tt
                    def v_stat(E, oc=oc, yc=yc, i2=i2):
                        xc = xrs[:, (yc % 2) * 512:(yc % 2) * 512 + 512]
                        sq = xch[:, (yc % 2) * 512:(yc % 2) * 512 + 512]
                        sa = sa0 if oc == 0 else sa1
                        sb = sb0 if oc == 0 else sb1
                        E.reduce_sum(sa[:, i2:i2 + 1], xc, axis=AX)
                        E.tensor_mul(sq, xc, xc)
                        return E.reduce_sum(sb[:, i2:i2 + 1], sq, axis=AX)
                    pb.op("vector", v_stat, inc="cv", waits=[])
                    pb.mark(f"v_sx{b_}_{yc}", "cv")

                    pb.dma("sync", lambda E, b_=b_, tt=tt, oc=oc, yc=yc: E.dma_start(
                        out=scr_xres[b_, tt * 128:(tt + 1) * 128, oc * 512:(oc + 1) * 512],
                        in_=xrs[:, (yc % 2) * 512:(yc % 2) * 512 + 512]),
                        waits=[f"v_sx{b_}_{yc}"], sem="cdx0" if yc % 2 == 0 else "cdx1")
                    pb.mark(f"d_xr{b_}_{yc}", "cdx0" if yc % 2 == 0 else "cdx1")
                    if d_xres is not None:
                        pb.dma("sync", lambda E, b_=b_, tt=tt, oc=oc, yc=yc: E.dma_start(
                            out=d_xres[b_, tt * 128:(tt + 1) * 128, oc * 512:(oc + 1) * 512],
                            in_=xrs[:, (yc % 2) * 512:(yc % 2) * 512 + 512]),
                            waits=[f"v_sx{b_}_{yc}"], sem="dg0" if yc % 2 == 0 else "dg1")
                        pb.mark(f"dbg_xr{b_}_{yc}", "dg0" if yc % 2 == 0 else "dg1")

            if d_otn is not None:
                pb.dma("gpsimd", lambda E, b_=b_: E.dma_start(
                    out=d_otn[b_], in_=otn128[:, :]),
                    waits=[f"v_noA{b_ * NP + NP - 1}", f"ld_mv{b_ * NP + NP - 1}",
                           f"ld_mv{b_ * NP + NP - 2}"], sem="stg")
                pb.mark(f"dbg_otn{b_}", "stg")
            pb.mark(f"t_avd{b_}", "ct")
            if b_ + 2 < B:
                kv_load(b_ + 2)

        for GG in range(NGG + 6):
            if GG < NGG:
                hhp, g = GG // NG, GG % NG
                b_, hp = hhp // NP, hhp % NP
                buf = b_ % 2
                dt = hp
                bank_s = pA if GG % 2 == 0 else pB
                w = [f"ld_kk{b_}_{NCORE - 1}", "s_biasC"]
                if GG >= 2:
                    w.append(f"s_e{GG - 2}")
                def t_sp(E, g=g, dt=dt, b_=b_, buf=buf, bank_s=bank_s):
                    r = None
                    for jj in range(GRP):
                        jt = g * GRP + jj
                        for hof, soff in ((0, 0), (64, 512)):
                            r = E.matmul(
                                bank_s[:, soff + jj * 256:soff + jj * 256 + 256],
                                kv_k[buf][hof:hof + 64,
                                          dt * T + jt * 128: dt * T + jt * 128 + 128],
                                qT[hof:hof + 64,
                                   dt * T + b_ * PL: dt * T + b_ * PL + PL],
                                start=True, stop=True)
                    return r
                pb.op("tensor", t_sp, inc="ct", waits=w)
                pb.mark(f"t_sp{GG}", "ct")

            if 1 <= GG <= NGG:
                emit_av(GG - 1)
                if (GG - 1) % NG == NG - 1:
                    pair_tail_a((GG - 1) // NG)

            if GG >= 5 and (GG - 5) % NG == NG - 1:
                hhp_p = (GG - 5) // NG
                rest_tail(hhp_p)
                if hhp_p % NP == NP - 1:
                    batch_tail(hhp_p // NP)

            if GG < NGG:
                hhp, g = GG // NG, GG % NG
                hp = hhp % NP
                b_ = hhp // NP
                bank_s = pA if GG % 2 == 0 else pB
                eoff = (GG % 4) * 1024
                w = [f"t_sp{GG}"]
                if GG >= 4:
                    w += [f"t_aA{GG - 4}", f"t_aB{GG - 4}"]
                pb.op("scalar", lambda E, bank_s=bank_s, eoff=eoff, hp=hp, b_=b_: E.activation(
                    EbufM[:, eoff:eoff + 1024], bank_s[:, :], AF.Exp,
                    bias=biasC[:, hp * 8 + b_:hp * 8 + b_ + 1], scale=0.125),
                    inc="cs", waits=w)
                pb.mark(f"s_e{GG}", "cs")

        pb.mark("cdx0_all", "cdx0")
        pb.mark("cdx1_all", "cdx1")
        pb.mark("dg0_all", "dg0")
        pb.mark("dg1_all", "dg1")

        def v_fin2(E):
            E.tensor_add(sc1[:, :], sa0[:, :], sa1[:, :])
            E.tensor_add(sc2[:, :], sb0[:, :], sb1[:, :])
            E.tensor_scalar(st2m[:, :], sc1[:, :], 1.0 / D, None, ALU.mult)
            E.tensor_scalar(sa0[:, :], sc2[:, :], 1.0 / D, None, ALU.mult)
            E.tensor_mul(sa1[:, :], st2m[:, :], st2m[:, :])
            E.memset(vspc[:, :], 0.0)
            E.tensor_sub(sb0[:, :], sa0[:, :], sa1[:, :])
            E.memset(vspc[:, :], 1.0)
            return E.tensor_scalar(sb1[:, :], sb0[:, :], LN_EPS, None, ALU.add)
        pb.op("vector", v_fin2, inc="cv", waits=[f"v_sx{B - 1}_3"])
        pb.mark("v_fin2", "cv")
        def s_rstd2(E):
            E.activation(sc2[:, :], sb1[:, :], AF.Ln)
            E.memzero(sspc[:, :])
            return E.activation(st2r[:, :], sc2[:, :], AF.Exp, scale=-0.5)
        pb.op("scalar", s_rstd2, inc="cs", waits=["v_fin2"])
        pb.mark("s_rstd2", "cs")

        if phases < 5:
            pb.wait("gpsimd", [("sto", pb.counts["sto"]), ("stg", pb.counts["stg"]),
                               ("cs", pb.counts["cs"]),
                               ("cv", pb.counts["cv"]), ("cdx0", pb.counts["cdx0"]), ("cdx1", pb.counts["cdx1"]),
                               ("cdc", pb.counts["cdc"])])
            return

        for i in range(6):
            pb.dma("gpsimd", lambda E, i=i: E.dma_start(
                out=Wg[i][:, :].rearrange("p (a c) -> p a c", a=ND),
                in_=agw[i].rearrange("a p c -> p a c")),
                waits=(["agw_done", f"t_avd{B - 1}"] if i == 0 else ()), sem="lwg")
            pb.mark(f"lw_g{i}", "lwg")

        for b_ in range(B):
            for tt in range(2):
                i2 = b_ * 2 + tt
                w = ["cdx0_all", "cdx1_all"] + (["dg0_all", "dg1_all"] if debug else [])
                if i2 >= 2:
                    w.append(f"v_ap2{i2 - 2}")
                else:
                    w.append(f"v_ye{B - 1}_3")
                pb.dma("sync", lambda E, b_=b_, tt=tt: E.dma_start(
                    out=xrl[:, tt * D:(tt + 1) * D],
                    in_=scr_xres[b_, tt * 128:(tt + 1) * 128, :]), waits=w,
                    sem="ldr0" if i2 % 2 == 0 else "ldr1")
                pb.mark(f"ld_x2{i2}", "ldr0" if i2 % 2 == 0 else "ldr1")
                xn = xnst0 if i2 % 2 == 0 else xnst1
                w = [f"ld_x2{i2}", "s_rstd2"]
                if i2 >= 2:
                    w.append(f"t_x2t{i2 - 2}")
                pb.op("vector", lambda E, i2=i2, tt=tt, xn=xn: E.tensor_scalar(
                    xn[:, :], xrl[:, tt * D:(tt + 1) * D], st2m[:, i2:i2 + 1],
                    st2r[:, i2:i2 + 1], ALU.subtract, ALU.mult),
                    inc="cv", waits=w)
                pb.mark(f"v_ap2{i2}", "cv")
                w = [f"v_ap2{i2}"]
                if i2 >= 1:
                    w.append(f"s_t2e{i2 - 1}")
                else:
                    w.append(f"v_noB{B * (H // 2) - 1}")
                    w.append(f"v_ye{B - 1}_3")
                def t_x2t(E, xn=xn):
                    r = None
                    for dt in range(ND):
                        pt = pC if dt < 4 else pD
                        r = E.matmul(pt[:, (dt % 4) * 128:(dt % 4) * 128 + 128],
                                     xn[:, dt * 128:(dt + 1) * 128], ident[:, :],
                                     start=True, stop=True)
                    return r
                pb.op("tensor", t_x2t, inc="ct", waits=w)
                pb.mark(f"t_x2t{i2}", "ct")
                def s_t2e(E, i2=i2):
                    v = xnT_dst(xn2T, i2)
                    E.copy(v[:, 0:4, :], pC[:, :].rearrange("p (a c) -> p a c", a=4))
                    return E.copy(v[:, 4:8, :], pD[:, :].rearrange("p (a c) -> p a c", a=4))
                pb.op("scalar", s_t2e, inc="cs", waits=[f"t_x2t{i2}"])
                pb.mark(f"s_t2e{i2}", "cs")

        if d_xn2T is not None:
            pb.dma("gpsimd", lambda E: E.dma_start(out=d_xn2T, in_=xn2T[:, :]),
                   waits=[f"s_t2e{2 * B - 1}"], sem="stg")
        pb.mark("stg_all", "stg")
        if phases < 6:
            pb.wait("gpsimd", [("sto", pb.counts["sto"]), ("stg", pb.counts["stg"]),
                               ("cs", pb.counts["cs"]),
                               ("cv", pb.counts["cv"]), ("cdx0", pb.counts["cdx0"]), ("cdx1", pb.counts["cdx1"]),
                               ("cdc", pb.counts["cdc"])])
            return

        pb.dma("sync", lambda E: E.dma_start(
            out=bcol[:, :].rearrange("p (v a) -> p v a", v=4),
            in_=bvec_in.rearrange("v (a p) -> p v a", p=128),
            allow_slow_non_contiguous=True),
            waits=[f"t_avd{B - 1}"], sem="ldb")
        pb.dma("sync", lambda E: E.dma_start(
            out=inccol[:, :], in_=inc_in.rearrange("(a p) -> p a", p=128),
            allow_slow_non_contiguous=True), sem="ldb")
        pb.mark("ld_inc", "ldb")
        pb.marks["ld_bcol"] = pb.marks["ld_inc"]
        pb.op("gpsimd", lambda E: E.memset(ones2[:, :], 1.0), inc="cg",
              waits=[f"t_avd{B - 1}"])
        pb.mark("g_ones2", "cg")

        def v_c0(E):
            r = None
            for ct in range(ND):
                E.tensor_scalar(c_st[0][:, ct * PL:(ct + 1) * PL], ones2[:, :],
                                inccol[:, ct:ct + 1], None, ALU.mult)
                r = E.tensor_scalar(c_bf0[:, ct * PL:(ct + 1) * PL], ones2[:, :],
                                    inccol[:, ct:ct + 1], None, ALU.mult)
            return r
        pb.op("vector", v_c0, inc="cv",
              waits=["ld_inc", "g_ones2", f"v_ap2{2 * B - 1}", f"v_ye{B - 1}_3", "stg_all"])
        pb.mark("v_cup_-1_7", "cv")

        for t in range(B):
            cprev = c_st[t % 2]
            cnew = c_st[1 - t % 2]
            cbfprev = c_bf0 if t % 2 == 0 else c_bf1
            cbfnew = c_bf1 if t % 2 == 0 else c_bf0
            for ct in range(ND):
                if ct % 2 == 0:
                    gdst = [pA[:, 0:256], pA[:, 512:768], pB[:, 0:256]]
                else:
                    gdst = [pB[:, 512:768], pC[:, 0:256], pD[:, 0:256]]
                w = ["lw_g5", f"s_t2e{2 * t + 1}"]
                if t == 0 and ct == 0:
                    w.append(f"s_e{B * (H // 2) * NG - 1}")
                    w.append(f"s_e{B * (H // 2) * NG - 2}")
                if t == 0 and ct == 1:
                    w.append(f"s_t2e{2 * B - 1}")
                if t > 0 or ct > 1:
                    pv = (t, ct - 2) if ct >= 2 else (t - 1, ct + 6)
                    w.append(f"s_g{pv[0]}_{pv[1]}")
                if d_g00 is not None and t == 0 and ct == 2:
                    w.append("dbg_g00")
                def t_wx(E, gdst=gdst, t=t, ct=ct):
                    r = None
                    for gi in range(3):
                        dstp = gdst[gi]
                        for dt in range(ND):
                            r = E.matmul(dstp,
                                         Wg[gi][:, dt * D + ct * 128: dt * D + ct * 128 + 128],
                                         xn2T[:, dt * T + t * PL: dt * T + t * PL + PL],
                                         start=(dt == 0), stop=False)
                    return r
                pb.op("tensor", t_wx, inc="ct", waits=w)
                pb.mark(f"t_wx{t}_{ct}", "ct")

                w = [f"v_cup_{t - 1}_7", "lw_g5"]
                def t_uc(E, gdst=gdst, ct=ct, cbfprev=cbfprev):
                    r = None
                    for gi in range(3):
                        dstp = gdst[gi]
                        for dt in range(ND):
                            r = E.matmul(dstp,
                                         Wg[3 + gi][:, dt * D + ct * 128: dt * D + ct * 128 + 128],
                                         cbfprev[:, dt * PL:(dt + 1) * PL],
                                         start=False, stop=(dt == ND - 1))
                    return r
                pb.op("tensor", t_uc, inc="ct", waits=w)
                pb.mark(f"t_uc{t}_{ct}", "ct")

                def s_g(E, gdst=gdst, ct=ct):
                    E.activation(ftb[:, :], gdst[0], AF.Sigmoid,
                                 bias=bcol[:, 0 * 8 + ct:0 * 8 + ct + 1])
                    E.activation(itb[:, :], gdst[1], AF.Sigmoid,
                                 bias=bcol[:, 1 * 8 + ct:1 * 8 + ct + 1])
                    return E.activation(gtb[:, :], gdst[2], AF.Tanh,
                                        bias=bcol[:, 2 * 8 + ct:2 * 8 + ct + 1])
                w = [f"t_uc{t}_{ct}", "ld_bcol"]
                if t > 0 or ct > 0:
                    pv = (t, ct - 1) if ct >= 1 else (t - 1, 7)
                    w.append(f"v_cup_{pv[0]}_{pv[1]}")
                if d_g00 is not None and t == 0 and ct == 1:
                    w.append("dbg_g00")
                pb.op("scalar", s_g, inc="cs", waits=w)
                pb.mark(f"s_g{t}_{ct}", "cs")
                if d_g00 is not None and t == 0 and ct == 0:
                    pb.dma("sync", lambda E: E.dma_start(out=d_g00[0], in_=ftb[:, :]),
                           waits=["s_g0_0"], sem="sto")
                    pb.dma("sync", lambda E: E.dma_start(out=d_g00[1], in_=itb[:, :]),
                           waits=["s_g0_0"], sem="sto")
                    pb.dma("sync", lambda E: E.dma_start(out=d_g00[2], in_=gtb[:, :]),
                           waits=["s_g0_0"], sem="sto")
                    def s_pre0cp(E):
                        r = None
                        for gi in range(3):
                            r = E.copy(xrl[:, gi * PL:(gi + 1) * PL], gdst[gi])
                        return r
                    pb.op("scalar", s_pre0cp, inc="cs", waits=["s_g0_0"])
                    pb.mark("s_pre0cp", "cs")
                    pb.dma("sync", lambda E: E.dma_start(
                        out=d_pre0.rearrange("g p c -> p g c"),
                        in_=xrl[:, 0:3 * PL].rearrange("p (g c) -> p g c", g=3)),
                        waits=["s_pre0cp"], sem="sto")
                    pb.dma("sync", lambda E: E.dma_start(out=d_wg0, in_=Wg[0][:, :]),
                           waits=["lw_g5"], sem="sto")
                    pb.mark("dbg_g00", "sto")

                def v_cup(E, ct=ct, cprev=cprev, cnew=cnew, cbfnew=cbfnew):
                    sl = slice(ct * PL, (ct + 1) * PL)
                    E.tensor_mul(g1s[:, :], ftb[:, :], cprev[:, sl])
                    E.tensor_mul(g2s[:, :], itb[:, :], gtb[:, :])
                    E.memset(vspc[:, :], 0.0)
                    E.tensor_add(cnew[:, sl], g1s[:, :], g2s[:, :])
                    E.memset(vspc[:, :], 1.0)
                    return E.tensor_copy(cbfnew[:, sl], cnew[:, sl])
                w = [f"s_g{t}_{ct}"]
                if t == 0 and ct == 0 and d_otn is not None:
                    w.append("stg_all")
                if d_c1 is not None and t == 2 and ct == 0:
                    w.append("dbg_c1")
                pb.op("vector", v_cup, inc="cv", waits=w)
                pb.mark(f"v_cup_{t}_{ct}", "cv")

        if d_c1 is not None:
            pb.dma("sync", lambda E: E.dma_start(out=d_c1, in_=c_st[1][:, :]),
                   waits=["v_cup_0_7"], sem="sto")
            pb.mark("dbg_c1", "sto")

        cf_bf = c_bf1 if (B - 1) % 2 == 0 else c_bf0
        cf_st = c_st[1 - (B - 1) % 2]
        if d_cf is not None:
            pb.dma("sync", lambda E: E.dma_start(out=d_cf, in_=cf_st[:, :]),
                   waits=[f"v_cup_{B - 1}_7"], sem="sto")
        if phases < 7:
            pb.wait("gpsimd", [("sto", pb.counts["sto"]), ("stg", pb.counts["stg"]),
                               ("cs", pb.counts["cs"]),
                               ("cv", pb.counts["cv"]), ("cdx0", pb.counts["cdx0"]), ("cdx1", pb.counts["cdx1"]),
                               ("cdc", pb.counts["cdc"])])
            return

        for i, nm in enumerate(["uo", "wo"]):
            dst = uoT if nm == "uo" else woT
            pb.dma("gpsimd", lambda E, i=i, dst=dst: E.dma_start(
                out=dst[:, :].rearrange("p (a c) -> p a c", a=ND),
                in_=agw[7 - i].rearrange("a p c -> p a c")),
                waits=([f"t_uc{B - 1}_7"] if i == 0 else ()), sem="lwg")
            pb.mark(f"lw_{nm}", "lwg")

        pb.dma("sync", lambda E: E.dma_start(out=borow2[0:1, :], in_=bvec_in[3:4, :]),
               waits=[f"t_y{B - 1}_3"], sem="ldn")
        pb.mark("ld_bo", "ldn")
        for oc in range(2):
            w = ["ld_bo", f"s_g{B - 1}_7"] if oc == 0 else ["s_bo0"]
            pb.op("tensor", lambda E, oc=oc: E.matmul(
                pD[:, :], onescol[0:1, :], borow2[0:1, oc * 512:(oc + 1) * 512],
                start=True, stop=True), inc="ct", waits=w)
            pb.mark(f"t_bo{oc}", "ct")
            pb.op("scalar", lambda E, oc=oc: E.copy(bo_b[:, oc * 512:(oc + 1) * 512],
                                                    pD[:, :]), inc="cs", waits=[f"t_bo{oc}"])
            pb.mark(f"s_bo{oc}", "cs")

        for pt_ in range(2):
            for oc in range(2):
                ch = pt_ * 2 + oc
                w = ["lw_wo", f"v_cup_{B - 1}_7", "s_bo1"]
                if ch >= 1:
                    w.append(f"v_cuo{ch - 1}")
                def t_cuo(E, pt_=pt_, oc=oc, cf_bf=cf_bf):
                    r = None
                    for dt in range(ND):
                        r = E.matmul(pC[:, :],
                                     cf_bf[:, dt * PL + pt_ * 128: dt * PL + pt_ * 128 + 128],
                                     uoT[:, dt * D + oc * 512: dt * D + oc * 512 + 512],
                                     start=(dt == 0), stop=(dt == ND - 1))
                    return r
                pb.op("tensor", t_cuo, inc="ct", waits=w)
                pb.mark(f"t_cuo{ch}", "ct")
                pb.op("vector", lambda E, pt_=pt_, oc=oc: E.tensor_add(
                    cuo_bo[:, pt_ * D + oc * 512: pt_ * D + oc * 512 + 512],
                    pC[:, :], bo_b[:, oc * 512:(oc + 1) * 512]),
                    inc="cv", waits=[f"t_cuo{ch}", f"v_cup_{B - 1}_7"])
                pb.mark(f"v_cuo{ch}", "cv")

        nst = 0
        for t in range(B):
            for tt in range(2):
                w = []
                if t >= 1:
                    w.append(f"v_o{t - 1}_{tt * 2 + 1}")
                else:
                    w.append(f"v_ap2{2 * B - 1}")
                    if d_g00 is not None:
                        w.append("dbg_g00")
                pb.dma("sync", lambda E, t=t, tt=tt: E.dma_start(
                    out=xrl[:, tt * D:(tt + 1) * D],
                    in_=scr_xres[t, tt * 128:(tt + 1) * 128, :]), waits=w,
                    sem="ldr0" if tt == 0 else "ldr1")
                pb.mark(f"ld_x3{t}_{tt}", "ldr0" if tt == 0 else "ldr1")
            for pt_ in range(2):
                for oc in range(2):
                    ch = pt_ * 2 + oc
                    ph = pC if nst % 2 == 0 else pD
                    w = ["lw_wo", "v_cuo3"]
                    if nst >= 2:
                        w.append(f"s_h{nst - 2}")
                    def t_h(E, ph=ph, pt_=pt_, oc=oc, t=t):
                        r = None
                        for dt in range(ND):
                            r = E.matmul(ph[:, :],
                                         xn2T[:, dt * T + t * PL + pt_ * 128:
                                              dt * T + t * PL + pt_ * 128 + 128],
                                         woT[:, dt * D + oc * 512: dt * D + oc * 512 + 512],
                                         start=(dt == 0), stop=(dt == ND - 1))
                        return r
                    pb.op("tensor", t_h, inc="ct", waits=w)
                    pb.mark(f"t_h{nst}", "ct")

                    w = [f"t_h{nst}"]
                    if nst >= 2:
                        w.append(f"d_on{nst - 2}")
                    pb.op("vector", lambda E, ph=ph, pt_=pt_, oc=oc, nst=nst: E.tensor_add(
                        preslot[:, (nst % 2) * 512:(nst % 2) * 512 + 512], ph[:, :],
                        cuo_bo[:, pt_ * D + oc * 512: pt_ * D + oc * 512 + 512]),
                        inc="cv", waits=w)
                    pb.mark(f"v_h{nst}", "cv")

                    w = [f"v_h{nst}"]
                    if nst >= 2:
                        w.append(f"v_o{t if ch >= 2 else t - 1}_{(ch - 2) % 4}")
                    pb.op("scalar", lambda E, nst=nst: E.activation(
                        tanslot[:, (nst % 2) * 512:(nst % 2) * 512 + 512],
                        preslot[:, (nst % 2) * 512:(nst % 2) * 512 + 512], AF.Tanh),
                        inc="cs", waits=w)
                    pb.mark(f"s_h{nst}", "cs")

                    pb.op("vector", lambda E, nst=nst, pt_=pt_, oc=oc: E.tensor_add(
                        preslot[:, (nst % 2) * 512:(nst % 2) * 512 + 512],
                        tanslot[:, (nst % 2) * 512:(nst % 2) * 512 + 512],
                        xrl[:, pt_ * D + oc * 512: pt_ * D + oc * 512 + 512]),
                        inc="cv", waits=[f"s_h{nst}", f"ld_x3{t}_{pt_}"])
                    pb.mark(f"v_o{t}_{ch}", "cv")

                    pb.dma("sync", lambda E, t=t, pt_=pt_, oc=oc, nst=nst: E.dma_start(
                        out=out_t[t, pt_ * 128:(pt_ + 1) * 128, oc * 512:(oc + 1) * 512],
                        in_=preslot[:, (nst % 2) * 512:(nst % 2) * 512 + 512]),
                        waits=[f"v_o{t}_{ch}"], sem=f"so{nst % 2}")
                    pb.mark(f"d_on{nst}", f"so{nst % 2}")
                    nst += 1

        pb.wait("gpsimd", [("sto", pb.counts["sto"]), ("stg", pb.counts["stg"]),
                           ("so0", pb.counts["so0"]), ("so1", pb.counts["so1"]),
                           ("cdx0", pb.counts["cdx0"]), ("cdx1", pb.counts["cdx1"]),
                           ("cs", pb.counts["cs"]), ("cv", pb.counts["cv"])])

    with es:
        with nc.Block() as block:
            for eng in ENGINES:
                def run(E, eng=eng):
                    pb.begin(eng, E)
                    prog(pb)
                getattr(block, eng)(run)

    return nc, dbg


def make_inputs(ins, core):
    x = np.ascontiguousarray(ins["L0_attn_x"][:, 256 * core:256 * (core + 1), :])
    trow = np.tile(np.arange(256, dtype=np.float32) + 256 * core, B)[None, :]
    wffn_names = ["L0_ffn_wf", "L0_ffn_wi", "L0_ffn_wc", "L0_ffn_uf",
                  "L0_ffn_ui", "L0_ffn_uc", "L0_ffn_wo", "L0_ffn_uo"]
    bvec = np.stack([ins["L0_ffn_bf"], ins["L0_ffn_bi"], ins["L0_ffn_bc"],
                     ins["L0_ffn_bo"]], axis=0)
    return {
        "x": x,
        "wqkv": np.asarray(ins["L0_attn_w_qkv"]),
        "wout": np.asarray(ins["L0_attn_w_out"]),
        "bout": np.asarray(ins["L0_attn_b_out"])[None, :],
        "wffn": np.asarray(ins[wffn_names[core]]),
        "bvec": np.ascontiguousarray(bvec),
        "incv": np.asarray(ins["L0_ffn_in_c"]),
        "trow": np.ascontiguousarray(trow),
    }


def kernel(**inputs):
    nc, _ = build(debug=0, phases=99)
    in_maps = [make_inputs(inputs, c) for c in range(NCORE)]
    res = run_bass_kernel_spmd(nc, in_maps, core_ids=list(range(NCORE)))
    return np.concatenate([r["out"] for r in res.results], axis=1)





# version 10
# speedup vs baseline: 1.0109x; 1.0109x over previous
import os
import numpy as np
from contextlib import ExitStack

import concourse.bass as bass
import concourse.mybir as mybir
from concourse.bass_utils import run_bass_kernel_spmd

F32 = mybir.dt.float32
BF16 = mybir.dt.bfloat16
I32 = mybir.dt.int32
AF = mybir.ActivationFunctionType
ALU = mybir.AluOpType
AX = mybir.AxisListType.X

NCORE = 8
B = 8
PL = 256
T = 2048
D = 1024
H = 16
NT = 16
ND = 8
TWO_PI = float(2 * np.pi)
PI = float(np.pi)
LN_EPS = 1e-5
SBASE = 16512

ENGINES = ("gpsimd", "sync", "tensor", "scalar", "vector")



CHAINED_OPS = {
    "memset", "memzero", "tensor_copy", "tensor_tensor", "tensor_add",
    "tensor_sub", "tensor_mul", "tensor_max", "tensor_scalar",
    "tensor_single_scalar", "tensor_scalar_add", "tensor_scalar_sub",
    "tensor_scalar_mul", "tensor_scalar_max", "tensor_scalar_min",
    "tensor_reduce", "reduce_sum", "reduce_max", "reciprocal", "activation",
    "copy", "add", "mul", "square", "sqrt", "iota", "affine_select",
    "bn_stats", "bn_aggr", "select", "stream_shuffle", "scalar_tensor_tensor",
    "sign", "tensor_relu", "transpose", "partition_broadcast",
}


class Chain:

    def __init__(self, E, sems, semname):
        self._E = E
        self._sem = sems[semname]
        self._prev = None
        self._edge = None
        self._n = 0

    def _pre(self):
        if self._prev is not None:
            if self._edge is None:
                self._n += 1
                self._prev.then_inc(self._sem, 1)
                self._edge = (self._sem, self._n)
            self._E.wait_ge(self._edge[0], self._edge[1])

    def _emit(self, name, *a, **k):
        self._pre()
        inst = getattr(self._E, name)(*a, **k)
        self._prev = inst
        self._edge = None
        return inst

    def note_inc(self, inst, sem, val):
        if inst is self._prev:
            self._edge = (sem, val)

    def __getattr__(self, name):
        if name in CHAINED_OPS:
            return lambda *a, **k: self._emit(name, *a, **k)
        return getattr(self._E, name)


class PB:
    def __init__(self, nc, sems):
        self.nc = nc
        self.sems = sems
        self.cur = None
        self.E = None
        self.counts = {k: 0 for k in sems}
        self.marks = {}

    def begin(self, eng, obj):
        self.cur = eng
        if eng in ("scalar", "vector", "gpsimd"):
            self.E = Chain(obj, self.sems, "cc_" + eng)
        else:
            self.E = obj
        self.counts = {k: 0 for k in self.sems}

    def op(self, eng, fn, inc=None, iv=1, waits=()):
        if self.cur == eng:
            best = {}
            for w in waits:
                s, v = self.marks[w] if isinstance(w, str) else w
                if v > 0:
                    best[s] = max(best.get(s, 0), v)
            for s, v in best.items():
                self.E.wait_ge(self.sems[s], v)
            inst = fn(self.E)
            if inc is not None:
                assert inst is not None, f"op for {eng} returned None with inc={inc}"
                inst.then_inc(self.sems[inc], iv)
                if isinstance(self.E, Chain):
                    self.E.note_inc(inst, self.sems[inc], self.counts[inc] + iv)
        if inc is not None:
            self.counts[inc] += iv

    def mark(self, name, sem):
        self.marks[name] = (sem, self.counts[sem])

    def dma(self, eng, fn, waits=(), sem="ld"):
        self.op(eng, fn, inc=sem, iv=16, waits=waits)

    def wait(self, eng, waits):
        if self.cur == eng:
            best = {}
            for w in waits:
                s, v = self.marks[w] if isinstance(w, str) else w
                if v > 0:
                    best[s] = max(best.get(s, 0), v)
            for s, v in best.items():
                self.E.wait_ge(self.sems[s], v)


def build(debug=1, phases=99):
    nc = bass.Bass("TRN2", target_bir_lowering=False, detect_race_conditions=True)

    x_in = nc.dram_tensor("x", [B, PL, D], F32, kind="ExternalInput").ap()
    wqkv = nc.dram_tensor("wqkv", [D, 3 * D], F32, kind="ExternalInput").ap()
    wout_in = nc.dram_tensor("wout", [D, D], F32, kind="ExternalInput").ap()
    bout_in = nc.dram_tensor("bout", [1, D], F32, kind="ExternalInput").ap()
    wffn = nc.dram_tensor("wffn", [D, D], F32, kind="ExternalInput").ap()
    bvec_in = nc.dram_tensor("bvec", [4, D], F32, kind="ExternalInput").ap()
    inc_in = nc.dram_tensor("incv", [D], F32, kind="ExternalInput").ap()
    trow_in = nc.dram_tensor("trow", [1, T], F32, kind="ExternalInput").ap()

    out_t = nc.dram_tensor("out", [B, PL, D], F32, kind="ExternalOutput").ap()

    cin_k = nc.dram_tensor("cin_k", [B, ND, 128, PL], BF16).ap()
    cin_v = nc.dram_tensor("cin_v", [B, 2, 128, D], BF16).ap()
    cin_w = nc.dram_tensor("cin_w", [ND, 128, D], BF16).ap()
    cin_kn = nc.dram_tensor("cin_kn", [16, 8], F32).ap()
    agk = nc.dram_tensor("agk", [NCORE, B, ND, 128, PL], BF16, addr_space="Shared").ap()
    agv = nc.dram_tensor("agv", [B, NCORE, 2, 128, D], BF16, addr_space="Shared").ap()
    agw = nc.dram_tensor("agw", [NCORE, ND, 128, D], BF16, addr_space="Shared").ap()
    agkn = nc.dram_tensor("agkn", [NCORE, 16, 8], F32, addr_space="Shared").ap()
    crow_d = nc.dram_tensor("crow_d", [16, 8], F32).ap()
    scr_xres = nc.dram_tensor("scr_xres", [B, PL, D], F32).ap()

    dbg = {}

    def dbgt(name, shape, cond=True, dtype=F32):
        if debug and cond:
            dbg[name] = nc.dram_tensor("d_" + name, shape, dtype, kind="ExternalOutput").ap()
        return dbg.get(name)

    def AT(name, shape, dtype, kb):
        return nc.alloc_sbuf_tensor_at(name, shape, dtype, offset=SBASE + (int(kb * 1024) + 31) // 32 * 32)

    ident = AT("ident", [128, 128], BF16, 0)
    pt2 = AT("pt2", [128, 128], BF16, 0.25)
    bones = AT("bones", [128, 8 * 16], BF16, 0.5)
    frow = AT("frow", [1, 64], F32, 0.75)
    fcol = AT("fcol", [1, 128], F32, 1.0)
    onescol = AT("onescol", [128, 128], F32, 1.5)
    biasC = AT("biasC", [128, 128], F32, 2.0)
    st1m = AT("st1m", [128, 16], F32, 2.5)
    st1r = AT("st1r", [128, 16], F32, 2.5 + 1 / 16)
    st1a = AT("st1a", [128, 16], F32, 2.5 + 2 / 16)
    st1b = AT("st1b", [128, 16], F32, 2.5 + 3 / 16)
    st2m = AT("st2m", [128, 16], F32, 2.5 + 4 / 16)
    st2r = AT("st2r", [128, 16], F32, 2.5 + 5 / 16)
    sc1 = AT("sc1", [128, 16], F32, 2.5 + 6 / 16)
    sc2 = AT("sc2", [128, 16], F32, 2.5 + 7 / 16)
    sa0 = AT("sa0", [128, 16], F32, 2.5 + 8 / 16)
    sa1 = AT("sa1", [128, 16], F32, 2.5 + 9 / 16)
    sb0 = AT("sb0", [128, 16], F32, 2.5 + 10 / 16)
    sb1 = AT("sb1", [128, 16], F32, 2.5 + 11 / 16)
    knm = AT("knm", [16, 8], F32, 3.25)
    qnm = AT("qnm", [16, 8], F32, 3.25 + 1 / 16)
    cmb = AT("cmb", [16, 8], F32, 3.25 + 2 / 16)
    crow = AT("crow", [1, 128], F32, 3.25 + 4 / 16)
    kn8 = AT("kn8", [16, 64], F32, 3.75)
    bcol = AT("bcol", [128, 32], F32, 4.0)
    inccol = AT("inccol", [128, 8], F32, 4.15)
    vspc = AT("vspc", [1, 64], F32, 4.2)
    sspc = AT("sspc", [1, 64], F32, 4.2 + 8 / 1024)
    otn128 = AT("otn128", [128, ND * PL], BF16, 6)
    crow2 = AT("crow2", [1, 64], F32, 5.75)
    c_bf0 = AT("c_bf0", [128, ND * PL], BF16, 6)
    sin2 = AT("sin2", [128, T], F32, 10)
    cos2 = AT("cos2", [128, T], F32, 18)
    w_out_sb = AT("w_out_sb", [128, ND * D], BF16, 10)
    cuo_bo = AT("cuo_bo", [128, 2 * D], F32, 10)
    bo_b = AT("bo_b", [128, D], F32, 18)
    borow2 = AT("borow2", [1, D], F32, 22)
    rbf = AT("rbf", [128, 2 * 512], BF16, 26)
    EbufM = AT("EbufM", [128, 4 * 1024], BF16, 26)
    xnst0 = AT("xnst0", [128, D], BF16, 26)
    xnst1 = AT("xnst1", [128, D], BF16, 28)
    angA = AT("angA", [128, 512], F32, 29)
    angB = AT("angB", [128, 512], F32, 31)
    angC = AT("angC", [128, 512], F32, 33)
    angI = AT("angI", [128, 512], I32, 35)
    rr1 = AT("rr1", [128, 2 * 512], F32, 29)
    rr2 = AT("rr2", [128, 2 * 512], F32, 33)
    g1s = AT("g1s", [128, PL], F32, 29)
    g2s = AT("g2s", [128, PL], F32, 30)
    preslot = AT("preslot", [128, 2 * 512], F32, 29)
    tanslot = AT("tanslot", [128, 2 * 512], F32, 33)
    bo2 = AT("bo2", [128, 640], BF16, 4.5)
    nrm = AT("nrm", [128, 512], F32, 34)
    otnstage = AT("otnstage", [128, 512], BF16, 36)
    sq0 = AT("sq0", [128, D], F32, 37)
    sq1 = AT("sq1", [128, D], F32, 41)
    nsq0 = AT("nsq0", [128, T], BF16, 37)
    nsq1 = AT("nsq1", [128, T], BF16, 41)
    xch = AT("xch", [128, 2 * 512], F32, 37)
    xrs = AT("xrs", [128, 2 * 512], F32, 41)
    xrl = AT("xrl", [128, 2 * D], F32, 37)
    qT = AT("qT", [128, ND * T], BF16, 45)
    xn2T = AT("xn2T", [128, ND * T], BF16, 45)
    wnat = AT("wnat", [128, ND * D], BF16, 45)
    wTown = AT("wTown", [128, ND * D], BF16, 61)
    xt = AT("xt", [128, NT * D], F32, 77)
    kv_k = [AT("kv_k0", [128, ND * T], BF16, 77),
            AT("kv_k1", [128, ND * T], BF16, 109)]
    kTown = AT("kTown", [128, ND * T], BF16, 109)
    trow_sb = AT("trow_sb", [1, T], F32, 141)
    xnT = AT("xnT", [128, ND * T], BF16, 141)
    vown = AT("vown", [128, NT * D], BF16, 77)
    kv_v = [AT("kv_v0", [128, 16 * 1040], BF16, 141),
            AT("kv_v1", [128, 16 * 1040], BF16, 174)]
    wstrip = AT("wstrip", [128, 2 * ND * 512], BF16, 174)
    Wg = [AT(f"Wg{i}", [128, ND * D], BF16, 77 + 16 * i) for i in range(6)]
    uoT = AT("uoT", [128, ND * D], BF16, 77)
    woT = AT("woT", [128, ND * D], BF16, 93)
    c_st = [AT("c0", [128, ND * PL], F32, 173), AT("c1", [128, ND * PL], F32, 181)]
    c_bf1 = AT("c_bf1", [128, ND * PL], BF16, 189)
    ftb = AT("ftb", [128, PL], F32, 193)
    itb = AT("itb", [128, PL], F32, 194)
    gtb = AT("gtb", [128, PL], F32, 195)
    ones2 = AT("ones2", [128, 256], F32, 196)

    es = ExitStack()
    pA = es.enter_context(nc.psum_tensor("pA", [128, 1024], F32))
    pB = es.enter_context(nc.psum_tensor("pB", [128, 1024], F32))
    pC = es.enter_context(nc.psum_tensor("pC", [128, 512], F32))
    pD = es.enter_context(nc.psum_tensor("pD", [128, 512], F32))
    pE = es.enter_context(nc.psum_tensor("pE", [128, 512], F32))
    pF = es.enter_context(nc.psum_tensor("pF", [128, 512], F32))

    semnames = ["lda", "ldx", "lw0", "lw1", "lwf", "lwo", "lwg", "ldk0", "ldk1", "ldv0", "ldv1",
    "ldc0", "ldc1", "ldm0", "ldm1", "ldr0", "ldr1", "ldb", "ldn", "cdk", "cdv", "cdc", "cdx0", "cdx1",
    "cg", "ct", "cs", "cv", "cc", "sto", "stg", "dg0", "dg1", "so0", "so1", "cc_scalar", "cc_vector", "cc_gpsimd"]
    sems = {n: es.enter_context(nc.semaphore(n)) for n in semnames}
    pb = PB(nc, sems)

    strips = [("k", D), ("k", D + 512), ("v", 2 * D), ("v", 2 * D + 512),
              ("q", 0), ("q", 512)]

    d_xnT = dbgt("xnT", [128, ND * T], phases >= 1)
    d_st = dbgt("st", [128, 64], True)
    d_xt = dbgt("xt", [128, 2 * D], True)
    d_qT = dbgt("qT", [128, ND * T], phases >= 2)
    d_kT = dbgt("kT", [128, ND * T], phases >= 2)
    d_v = dbgt("v", [128, NT * D], phases >= 2)
    d_C = dbgt("C", [128, 128], phases >= 3)
    d_otn = dbgt("otn", [B, 128, ND * PL], phases >= 4)
    d_xres = dbgt("xres", [B, PL, D], phases >= 4)
    d_xn2T = dbgt("xn2T", [128, ND * T], phases >= 5)
    d_cf = dbgt("cf", [128, ND * PL], phases >= 6)
    d_g00 = dbgt("g00", [3, 128, PL], phases >= 6)
    d_c1 = dbgt("c1", [128, ND * PL], phases >= 6)
    d_pre0 = dbgt("pre0", [3, 128, PL], phases >= 6)
    d_wg0 = dbgt("wg0", [128, ND * D], phases >= 6, dtype=BF16)

    def xnT_dst(base, i):
        return base[:, 0:ND * T].rearrange("p (a c) -> p a c", a=ND)[:, :, i * 128:(i + 1) * 128]

    NG = 8
    GRP = 2

    def prog(pb):
        def g_c1(E):
            E.memset(ident[:, :], 0.0)
            E.memset(pt2[:, :], 0.0)
            E.memset(bones[:, :], 0.0)
            return E.memset(onescol[:, :], 0.0)
        pb.op("gpsimd", g_c1, inc="cg")
        pb.mark("g_c1", "cg")

        def g_c2(E):
            E.affine_select(out=ident[:, :], in_=ident[:, :], compare_op=ALU.not_equal,
                            fill=1.0, base=0, pattern=[[-1, 128]], channel_multiplier=1)
            for q in range(2):
                sub = pt2[64 * q:64 * q + 64, 64 * q:64 * q + 64]
                E.affine_select(out=sub, in_=sub, compare_op=ALU.not_equal,
                                fill=-1.0, base=-1, pattern=[[-2, 64]], channel_multiplier=1)
            for dt in range(8):
                E.memset(bones[0:64, dt * 16 + 2 * dt:dt * 16 + 2 * dt + 1], 1.0)
                E.memset(bones[64:128, dt * 16 + 2 * dt + 1:dt * 16 + 2 * dt + 2], 1.0)
            E.memset(onescol[64:65, :], 1.0)
            E.memset(onescol[0:1, :], 1.0)
            return E.iota(frow[0:1, 0:32], pattern=[[1, 32]], base=0,
                          channel_multiplier=0, allow_small_or_imprecise_dtypes=True)
        pb.op("gpsimd", g_c2, inc="cg", waits=["g_c1"])
        pb.mark("g_frow", "cg")

        def g_c3(E):
            r = None
            for q in range(2):
                sub = pt2[64 * q:64 * q + 64, 64 * q:64 * q + 64]
                r = E.affine_select(out=sub, in_=sub, compare_op=ALU.not_equal,
                                    fill=1.0, base=64, pattern=[[-2, 64]], channel_multiplier=1)
            return r
        pb.op("gpsimd", g_c3, inc="cg", waits=["g_frow"])
        pb.mark("g_consts", "cg")

        pb.dma("gpsimd", lambda E: E.dma_start(
            out=wnat[:, :].rearrange("p (a c) -> p a c", a=ND),
            in_=wffn.rearrange("(a p) c -> p a c", p=128)), sem="lwf")
        pb.mark("lw_ffn", "lwf")

        pb.dma("sync", lambda E: E.dma_start(out=trow_sb[:, :], in_=trow_in[:, :]), sem="lda")
        pb.mark("lda_all", "lda")
        pb.dma("sync", lambda E: E.dma_start(
            out=xt[:, :].rearrange("p (b t c) -> p b t c", b=B, t=2),
            in_=x_in.rearrange("b (t p) d -> p b t d", t=2)), sem="ldx")
        pb.mark("ldx_all", "ldx")

        pb.op("scalar", lambda E: E.activation(frow[0:1, 0:32], frow[0:1, 0:32], AF.Exp,
                                               scale=-float(np.log(10000.0)) / 32.0),
              inc="cs", waits=["g_consts"])
        pb.mark("s_frow", "cs")

        def v_fcol(E):
            E.tensor_copy(fcol[0:1, 0:64:2], frow[0:1, 0:32])
            E.tensor_copy(fcol[0:1, 1:64:2], frow[0:1, 0:32])
            E.tensor_copy(fcol[0:1, 64:128:2], frow[0:1, 0:32])
            return E.tensor_copy(fcol[0:1, 65:128:2], frow[0:1, 0:32])
        pb.op("vector", v_fcol, inc="cv", waits=["s_frow"])
        pb.mark("v_fcol", "cv")

        TBC = 256
        for c in range(8 if phases >= 1 else 0):
            ps = (pC if c % 2 == 0 else pD)[:, 0:TBC]
            w = ["v_fcol", "lda_all"] if c == 0 else ([f"v_tab{c - 2}"] if c >= 2 else [])
            pb.op("tensor", lambda E, ps=ps, c=c: E.matmul(
                ps, fcol[0:1, :], trow_sb[0:1, c * TBC:(c + 1) * TBC],
                start=True, stop=True), inc="ct", waits=w)
            pb.mark(f"t_tab{c}", "ct")

            def v_tab(E, ps=ps, c=c):
                sl = slice(c * TBC, (c + 1) * TBC)
                aS, aC = angA[:, 0:TBC], angA[:, TBC:2 * TBC]
                bS, bC = angB[:, 0:TBC], angB[:, TBC:2 * TBC]
                iS, iC = angC[:, 0:TBC], angC[:, TBC:2 * TBC]
                jS, jC = angI[:, 0:TBC], angI[:, TBC:2 * TBC]
                tS, tC = sin2[:, sl], cos2[:, sl]
                E.tensor_scalar(aS, ps, 0.0, None, ALU.add)
                E.tensor_scalar(aC, ps, PI / 2, None, ALU.add)
                E.tensor_scalar(bS, aS, 1.0 / TWO_PI, None, ALU.mult)
                E.tensor_scalar(bC, aC, 1.0 / TWO_PI, None, ALU.mult)
                E.tensor_copy(jS, bS)
                E.tensor_copy(jC, bC)
                E.tensor_copy(bS, jS)
                E.tensor_copy(bC, jC)
                E.tensor_scalar(tS, bS, TWO_PI, None, ALU.mult)
                E.tensor_scalar(tC, bC, TWO_PI, None, ALU.mult)
                E.tensor_sub(iS, aS, tS)
                E.tensor_sub(iC, aC, tC)
                E.tensor_scalar(bS, iS, PI, None, ALU.is_gt)
                E.tensor_scalar(bC, iC, PI, None, ALU.is_gt)
                E.tensor_scalar(tS, bS, TWO_PI, None, ALU.mult)
                E.tensor_scalar(tC, bC, TWO_PI, None, ALU.mult)
                E.tensor_sub(aS, iS, tS)
                E.tensor_sub(aC, iC, tC)
                E.tensor_scalar(bS, aS, -PI, None, ALU.is_lt)
                E.tensor_scalar(bC, aC, -PI, None, ALU.is_lt)
                E.tensor_scalar(tS, bS, TWO_PI, None, ALU.mult)
                E.tensor_scalar(tC, bC, TWO_PI, None, ALU.mult)
                E.tensor_add(iS, aS, tS)
                return E.tensor_add(iC, aC, tC)
            w = [f"t_tab{c}"]
            if c >= 1:
                w.append(f"s_tab{c - 1}")
            pb.op("vector", v_tab, inc="cv", waits=w)
            pb.mark(f"v_tab{c}", "cv")

            def s_tab(E, c=c):
                sl = slice(c * TBC, (c + 1) * TBC)
                E.activation(sin2[:, sl], angC[:, 0:TBC], AF.Sin)
                return E.activation(cos2[:, sl], angC[:, TBC:2 * TBC], AF.Sin)
            pb.op("scalar", s_tab, inc="cs", waits=[f"v_tab{c}"])
            pb.mark(f"s_tab{c}", "cs")
        if phases >= 1:
            pb.marks["s_tab7"] = pb.marks["s_tab7"]

        for g in range(ND):
            pt = pA if g % 2 == 0 else pB
            w = ["lw_ffn", "g_consts"]
            if g >= 2:
                w.append(f"s_wt{g - 2}")
            def t_wt(E, g=g, pt=pt):
                r = None
                for fa in range(ND):
                    r = E.matmul(pt[:, fa * 128:(fa + 1) * 128],
                                 wnat[:, fa * D + g * 128: fa * D + g * 128 + 128],
                                 ident[:, :], start=True, stop=True)
                return r
            pb.op("tensor", t_wt, inc="ct", waits=w)
            pb.mark(f"t_wt{g}", "ct")
            pb.op("scalar", lambda E, g=g, pt=pt: E.copy(
                wTown[:, g * D:(g + 1) * D], pt[:, :]), inc="cs", waits=[f"t_wt{g}"])
            pb.mark(f"s_wt{g}", "cs")

        pb.dma("sync", lambda E: E.dma_start(
            out=cin_w.rearrange("a p c -> p a c"),
            in_=wTown[:, :].rearrange("p (a c) -> p a c", a=ND)),
            waits=[f"s_wt{ND - 1}"], sem="cdk")
        pb.mark("d_wT", "cdk")

        for i in range(NT):
            sq = sq0 if i % 2 == 0 else sq1
            w = ["ldx_all"]
            if i >= 2:
                w.append(f"v_st{i - 2}")
            pb.op("scalar", lambda E, i=i, sq=sq: E.activation(
                sq[:, :], xt[:, i * D:(i + 1) * D], AF.Square), inc="cs", waits=w)
            pb.mark(f"s_sq{i}", "cs")

            def v_st(E, i=i, sq=sq):
                E.reduce_sum(st1a[:, i:i + 1], xt[:, i * D:(i + 1) * D], axis=AX)
                return E.reduce_sum(st1b[:, i:i + 1], sq[:, :], axis=AX)
            pb.op("vector", v_st, inc="cv", waits=[f"s_sq{i}"])
            pb.mark(f"v_st{i}", "cv")

        def v_fin1(E):
            E.tensor_scalar(sc1[:, :], st1a[:, :], 1.0 / D, None, ALU.mult)
            E.tensor_scalar(sa0[:, :], st1b[:, :], 1.0 / D, None, ALU.mult)
            E.tensor_mul(sc2[:, :], sc1[:, :], sc1[:, :])
            E.tensor_copy(st1m[:, :], sc1[:, :])
            E.tensor_sub(sa1[:, :], sa0[:, :], sc2[:, :])
            E.memset(vspc[:, :], 0.0)
            return E.tensor_scalar(sb0[:, :], sa1[:, :], LN_EPS, None, ALU.add)
        pb.op("vector", v_fin1, inc="cv", waits=[f"v_st{NT - 1}"])
        pb.mark("v_fin1", "cv")

        def s_rstd1(E):
            E.activation(sb1[:, :], sb0[:, :], AF.Ln)
            E.memzero(sspc[:, :])
            return E.activation(st1r[:, :], sb1[:, :], AF.Exp, scale=-0.5)
        pb.op("scalar", s_rstd1, inc="cs", waits=["v_fin1"])
        pb.mark("s_rstd1", "cs")

        for i in range(NT):
            xn = xnst0 if i % 2 == 0 else xnst1
            w = ["s_rstd1"]
            if i >= 2:
                w.append(f"t_tr{i - 2}")
            pb.op("vector", lambda E, i=i, xn=xn: E.tensor_scalar(
                xn[:, :], xt[:, i * D:(i + 1) * D], st1m[:, i:i + 1], st1r[:, i:i + 1],
                ALU.subtract, ALU.mult), inc="cv", waits=w)
            pb.mark(f"v_ap{i}", "cv")

            if phases < 1:
                pb.mark(f"t_tr{i}", "ct")
                pb.mark(f"s_te{i}", "cs")
                continue
            pt = pA if i % 2 == 0 else pB
            w = [f"v_ap{i}", "g_consts"]
            if i >= 2:
                w.append(f"s_te{i - 2}")
            else:
                w.append(f"s_wt{ND - 1}")
            def t_tr(E, xn=xn, pt=pt):
                r = None
                for dt in range(ND):
                    r = E.matmul(pt[:, dt * 128:(dt + 1) * 128],
                                 xn[:, dt * 128:(dt + 1) * 128], ident[:, :],
                                 start=True, stop=True)
                return r
            pb.op("tensor", t_tr, inc="ct", waits=w)
            pb.mark(f"t_tr{i}", "ct")

            wte = [f"t_tr{i}"] + (["t_tab7"] if (i == 0 and phases >= 1) else [])
            pb.op("scalar", lambda E, i=i, pt=pt: E.copy(
                xnT_dst(xnT, i), pt[:, :].rearrange("p (a c) -> p a c", a=ND)),
                inc="cs", waits=wte)
            pb.mark(f"s_te{i}", "cs")

        if d_xt is not None:
            pb.dma("sync", lambda E: E.dma_start(out=d_xt[:, 0:D], in_=xt[:, 0:D]),
                   waits=["ldx_all"], sem="sto")
            pb.dma("sync", lambda E: E.dma_start(out=d_xt[:, D:2 * D], in_=xt[:, D:2 * D]),
                   waits=["ldx_all"], sem="sto")

        if d_st is not None:
            for j, tsr in enumerate([st1m, st1r, st1a, st1b]):
                pb.dma("sync", lambda E, j=j, tsr=tsr: E.dma_start(
                    out=d_st[:, 16 * j:16 * (j + 1)], in_=tsr[:, :]),
                    waits=[f"s_te{NT - 1}"], sem="sto")
        pb.mark("dbg_xt", "sto")
        if d_xnT is not None and phases >= 1:
            pb.dma("gpsimd", lambda E: E.dma_start(out=d_xnT, in_=xnT[:, :]),
                   waits=[f"s_te{NT - 1}"], sem="stg")
        if phases < 2:
            pb.wait("gpsimd", [("sto", pb.counts["sto"]), ("stg", pb.counts["stg"]), ("cs", pb.counts["cs"]), ("cv", pb.counts["cv"])])
            return

        def strip_dma(s):
            kind, off = strips[s]
            buf = s % 2
            w = [f"t_strip{s - 2}"] if s >= 2 else []
            pb.dma("gpsimd", lambda E, off=off, buf=buf: E.dma_start(
                out=wstrip[:, buf * ND * 512:(buf + 1) * ND * 512]
                .rearrange("p (a c) -> p a c", a=ND),
                in_=wqkv[:, off:off + 512].rearrange("(a p) c -> p a c", p=128)),
                waits=w, sem="lw0" if s % 2 == 0 else "lw1")
            pb.mark(f"lw_strip{s}", "lw0" if s % 2 == 0 else "lw1")

        strip_dma(0)
        strip_dma(1)

        def norms(tag, srcT, outm, wait0):
            for dt in range(ND):
                nq = nsq0 if dt % 2 == 0 else nsq1
                w = list(wait0) if dt == 0 else ([f"t_ns{tag}{dt - 2}"] if dt >= 2 else [])
                pb.op("vector", lambda E, nq=nq, dt=dt, srcT=srcT: E.tensor_mul(
                    nq[:, :], srcT[:, dt * T:(dt + 1) * T], srcT[:, dt * T:(dt + 1) * T]),
                    inc="cv", waits=w)
                pb.mark(f"v_ns{tag}{dt}", "cv")
                def t_ns(E, nq=nq, dt=dt):
                    r = None
                    for tc in range(4):
                        ph = (pA if tc < 2 else pB)[0:16, (tc % 2) * 512:(tc % 2) * 512 + 512]
                        r = E.matmul(ph, bones[:, dt * 16:(dt + 1) * 16],
                                     nq[:, tc * 512:(tc + 1) * 512],
                                     start=(dt == 0), stop=(dt == ND - 1))
                    return r
                pb.op("tensor", t_ns, inc="ct", waits=[f"v_ns{tag}{dt}"])
                pb.mark(f"t_ns{tag}{dt}", "ct")
            def v_nred(E, outm=outm):
                r = None
                for tc in range(4):
                    ph = (pA if tc < 2 else pB)[0:16, (tc % 2) * 512:(tc % 2) * 512 + 512]
                    r = E.tensor_reduce(outm[:, tc * 2:tc * 2 + 2],
                                        ph.rearrange("p (a c) -> p a c", a=2),
                                        axis=AX, op=ALU.max)
                return r
            pb.op("vector", v_nred, inc="cv", waits=[f"t_ns{tag}{ND - 1}"])
            pb.mark(f"v_nred{tag}", "cv")

        ci = 0
        vi = 0
        for s, (kind, off) in enumerate(strips):
            buf = s % 2
            if kind in ("k", "q"):
                for fs in range(4):
                    for tc in range(4):
                        pah = pA[:, (ci % 2) * 512:(ci % 2) * 512 + 512]
                        pbh = pB[:, (ci % 2) * 512:(ci % 2) * 512 + 512]
                        w = [f"lw_strip{s}", f"s_te{NT - 1}"]
                        if ci <= 1:
                            w.append(f"s_wt{ND - 1}")
                        if kind == "q" and fs == 0 and tc == 0:
                            w.append("cdk_all")
                        if ci in (32, 33):
                            w.append("v_nredk")
                        if ci >= 2:
                            w.append(f"v_rope{ci - 2}")
                        def t_mm(E, pah=pah, fs=fs, tc=tc, buf=buf):
                            r = None
                            for dt in range(ND):
                                base = buf * ND * 512 + dt * 512 + fs * 128
                                r = E.matmul(pah, wstrip[:, base:base + 128],
                                             xnT[:, dt * T + tc * 512: dt * T + tc * 512 + 512],
                                             start=(dt == 0), stop=(dt == ND - 1))
                            return r
                        pb.op("tensor", t_mm, inc="ct", waits=w)
                        pb.mark(f"t_qkv{ci}", "ct")
                        rb = rbf[:, (ci % 2) * 512:(ci % 2) * 512 + 512]
                        pb.op("scalar", lambda E, rb=rb, pah=pah: E.copy(rb, pah),
                              inc="cs", waits=[f"t_qkv{ci}"])
                        pb.mark(f"s_kbf{ci}", "cs")
                        w2 = [f"s_kbf{ci}"]
                        if ci in (32, 33):
                            w2.append("v_nredk")
                        if ci >= 2:
                            w2.append(f"v_rope{ci - 2}")
                        pb.op("tensor", lambda E, pbh=pbh, rb=rb: E.matmul(
                            pbh, pt2[:, :], rb, start=True, stop=True),
                            inc="ct", waits=w2)
                        pb.mark(f"t_rot{ci}", "ct")
                        dstb = kTown if kind == "k" else qT
                        dtk = (off - (D if kind == "k" else 0)) // 128 + fs
                        dst = dstb[:, dtk * T + tc * 512: dtk * T + tc * 512 + 512]
                        def v_rope(E, pah=pah, pbh=pbh, dst=dst, tc=tc, ci=ci):
                            t1 = rr1[:, (ci % 2) * 512:(ci % 2) * 512 + 512]
                            t2 = rr2[:, (ci % 2) * 512:(ci % 2) * 512 + 512]
                            E.tensor_mul(t1, pah, cos2[:, tc * 512:(tc + 1) * 512])
                            E.tensor_mul(t2, pbh, sin2[:, tc * 512:(tc + 1) * 512])
                            E.memset(vspc[:, :], 0.0)
                            return E.tensor_add(dst, t1, t2)
                        pb.op("vector", v_rope, inc="cv", waits=[f"t_rot{ci}", "s_tab7"] + (["dbg_xt"] if ci == 0 and debug else []))
                        pb.mark(f"v_rope{ci}", "cv")
                        ci += 1
            else:
                half = (off - 2 * D) // 512
                for tt in range(NT):
                    ps = pC if vi % 2 == 0 else pD
                    w = [f"lw_strip{s}", f"s_te{NT - 1}", "s_tab7"]
                    if vi >= 2:
                        w.append(f"s_ve{vi - 2}")
                    def t_vm(E, ps=ps, tt=tt, buf=buf):
                        r = None
                        for dt in range(ND):
                            r = E.matmul(ps[:, :],
                                         xnT[:, dt * T + tt * 128: dt * T + tt * 128 + 128],
                                         wstrip[:, buf * ND * 512 + dt * 512:
                                                buf * ND * 512 + dt * 512 + 512],
                                         start=(dt == 0), stop=(dt == ND - 1))
                        return r
                    pb.op("tensor", t_vm, inc="ct", waits=w)
                    pb.mark(f"t_v{vi}", "ct")
                    pb.op("scalar", lambda E, ps=ps, tt=tt, half=half: E.copy(
                        vown[:, tt * D + half * 512: tt * D + half * 512 + 512], ps[:, :]),
                        inc="cs", waits=[f"t_v{vi}"] + (["dbg_xt"] if vi == 0 and debug else []))
                    pb.mark(f"s_ve{vi}", "cs")
                    vi += 1
            pb.mark(f"t_strip{s}", "ct")
            if s + 2 < len(strips):
                strip_dma(s + 2)

            if s == 1:
                for b_ in range(B):
                    pb.dma("sync", lambda E, b_=b_: E.dma_start(
                        out=cin_k[b_].rearrange("a p c -> p a c"),
                        in_=kTown[:, :].rearrange("p (a c) -> p a c", a=ND)
                        [:, :, b_ * PL:(b_ + 1) * PL]),
                        waits=([f"v_rope{31}"] if b_ == 0 else []), sem="cdk")
                    pb.mark(f"d_k{b_}", "cdk")
                norms("k", kTown, knm, [f"v_rope{31}"])
                pb.dma("sync", lambda E: E.dma_start(out=cin_kn[:, :], in_=knm[:, :]),
                       waits=["v_nredk"], sem="cdk")
                pb.mark("d_kn", "cdk")
                pb.mark("cdk_all", "cdk")
                pb.op("gpsimd", lambda E: E.collective_compute(
                    "AllGather", ALU.bypass, replica_groups=[list(range(NCORE))],
                    ins=[cin_k[:, :, :, :]], outs=[agk[:, :, :, :, :]]),
                    inc="cc", waits=["d_wT", "d_kn"] + [f"d_k{b_}" for b_ in range(B)])
                pb.op("gpsimd", lambda E: E.collective_compute(
                    "AllGather", ALU.bypass, replica_groups=[list(range(NCORE))],
                    ins=[cin_kn[:, :]], outs=[agkn[:, :, :]]), inc="cc")
                pb.mark("ag1", "cc")
            if s == 3:
                for b_ in range(B):
                    pb.dma("sync", lambda E, b_=b_: E.dma_start(
                        out=cin_v[b_].rearrange("j p c -> p j c"),
                        in_=vown[:, 2 * b_ * D:(2 * b_ + 2) * D]
                        .rearrange("p (a c) -> p a c", a=2)),
                        waits=([f"s_ve{31}"] if b_ == 0 else []), sem="cdv")
                    pb.mark(f"d_v{b_}", "cdv")
                pb.mark("d_v_all", "cdv")
                for b_ in range(B):
                    pb.op("gpsimd", lambda E, b_=b_: E.collective_compute(
                        "AllGather", ALU.bypass, replica_groups=[list(range(NCORE))],
                        ins=[cin_v[b_]], outs=[agv[b_]]),
                        inc="cc", waits=(["d_v_all"] if b_ == 0 else []))
                    pb.mark(f"ag2_{b_}", "cc")
                pb.op("gpsimd", lambda E: E.collective_compute(
                    "AllGather", ALU.bypass, replica_groups=[list(range(NCORE))],
                    ins=[cin_w[:, :, :]], outs=[agw[:, :, :, :]]), inc="cc")
                pb.mark("agw_done", "cc")

        norms("q", qT, qnm, [f"v_rope{63}"])

        if d_kT is not None:
            pb.dma("gpsimd", lambda E: E.dma_start(out=d_kT, in_=kTown[:, :]),
                   waits=[f"v_rope{31}"], sem="stg")
        if d_qT is not None:
            pb.dma("gpsimd", lambda E: E.dma_start(out=d_qT, in_=qT[:, :]),
                   waits=[f"v_rope{63}"], sem="stg")
        if d_v is not None:
            pb.dma("gpsimd", lambda E: E.dma_start(out=d_v, in_=vown[:, :]),
                   waits=[f"s_ve{31}"], sem="stg")
        pb.mark("dbg_xnT", "stg")
        if phases < 3:
            pb.wait("gpsimd", [("sto", pb.counts["sto"]), ("stg", pb.counts["stg"]),
                               ("cs", pb.counts["cs"]),
                               ("cv", pb.counts["cv"]), ("cc", pb.counts["cc"]),
                               ("cdk", pb.counts["cdk"]), ("cdv", pb.counts["cdv"])])
            return

        pb.dma("sync", lambda E: E.dma_start(
            out=kn8[:, :].rearrange("p (b r) -> p b r", b=8),
            in_=agkn.transpose([1, 2, 0]), allow_slow_non_contiguous=True),
            waits=["ag1"], sem="ldn")
        pb.mark("ld_kn8", "ldn")

        def v_kmax(E):
            E.tensor_reduce(knm[:, :], kn8[:, :].rearrange("p (b r) -> p b r", b=8),
                            axis=AX, op=ALU.max)
            E.memset(vspc[:, :], 0.0)
            E.tensor_add(cmb[:, :], knm[:, :], qnm[:, :])
            E.memset(vspc[:, :], 1.0)
            return E.tensor_scalar(qnm[:, :], cmb[:, :], 1.02 / 16.0, 0.05,
                                   ALU.mult, ALU.add)
        pb.op("vector", v_kmax, inc="cv", waits=["ld_kn8", "v_nredq"])
        pb.mark("v_cmb", "cv")
        pb.dma("sync", lambda E: E.dma_start(out=crow_d[:, :], in_=qnm[:, :]),
               waits=["v_cmb"], sem="cdc")
        pb.mark("d_cmb", "cdc")
        pb.dma("sync", lambda E: E.dma_start(
            out=crow[0:1, :], in_=crow_d.rearrange("a b -> (a b)")), waits=["d_cmb"], sem="ldn")
        pb.mark("ld_crow", "ldn")
        def v_cpair(E):
            cv = crow[0:1, :].rearrange("a (hp t b) -> a hp t b", hp=8, t=2)
            return E.tensor_max(crow2[0:1, :].rearrange("a (hp b) -> a hp b", hp=8),
                                cv[:, :, 0, :], cv[:, :, 1, :])
        pb.op("vector", v_cpair, inc="cv", waits=["ld_crow"])
        pb.mark("v_cpair", "cv")
        pb.op("tensor", lambda E: E.matmul(pE[:, 0:64], onescol[0:1, :], crow2[0:1, :],
                                           start=True, stop=True),
              inc="ct", waits=["v_cpair", f"s_te{NT - 1}", f"s_wt{ND - 1}"])
        pb.mark("t_biasC", "ct")
        pb.op("scalar", lambda E: E.activation(biasC[:, 0:64], pE[:, 0:64], AF.Copy,
                                               scale=-1.0), inc="cs", waits=["t_biasC"])
        pb.mark("s_biasC", "cs")
        if d_C is not None:
            pb.dma("sync", lambda E: E.dma_start(out=d_C[:, 0:64], in_=biasC[:, 0:64]),
                   waits=["s_biasC"], sem="sto")
        if phases < 4:
            pb.wait("gpsimd", [("sto", pb.counts["sto"]), ("stg", pb.counts["stg"]),
                               ("cs", pb.counts["cs"]),
                               ("cv", pb.counts["cv"]), ("cc", pb.counts["cc"]),
                               ("cdk", pb.counts["cdk"]), ("cdv", pb.counts["cdv"])])
            return

        pb.dma("gpsimd", lambda E: E.dma_start(
            out=w_out_sb[:, :].rearrange("p (a c) -> p a c", a=ND),
            in_=wout_in.rearrange("(a p) c -> p a c", p=128)),
            waits=[f"v_rope{63}"], sem="lwo")
        pb.mark("lw_wout", "lwo")

        pb.dma("gpsimd", lambda E: E.dma_start(
            out=bo2[0:1, 128:640], in_=bout_in[0:1, 0:512]), sem="lwf")
        pb.dma("gpsimd", lambda E: E.dma_start(
            out=bo2[64:65, 0:512], in_=bout_in[0:1, 512:1024]), sem="lwf")
        pb.mark("lw_bo2", "lwf")

        def g_vones(E):
            E.memset(bo2[0:1, 0:128], 1.0)
            E.memset(bo2[64:65, 512:640], 1.0)
            r = None
            for bf in range(2):
                r = E.memset(kv_v[bf][:, :].rearrange("p (j e) -> p j e", e=65)
                             [:, :, 64:65], 1.0)
            return r
        pb.op("gpsimd", g_vones, inc="cg",
              waits=[f"t_strip{5}", "v_nredq", f"d_v{B - 1}", "dbg_xnT"])
        pb.mark("g_vones", "cg")

        def kv_load(b_):
            buf = b_ % 2
            w = ["ag1", "g_vones"]
            if b_ == 0:
                w += [f"v_ap{NT - 1}"]
            if b_ == 1:
                w += ["v_nredq"]
            if b_ >= 2:
                w += [f"t_avd{b_ - 2}"]
            for r in range(NCORE):
                pb.dma("sync", lambda E, b_=b_, r=r, buf=buf: E.dma_start(
                    out=kv_k[buf][:, :].rearrange("p (a g c) -> p a g c", a=ND, g=NCORE)
                    [:, :, r, :],
                    in_=agk[r, b_].rearrange("a p c -> p a c")),
                    waits=(w if r == 0 else ()), sem="ldk0" if b_ % 2 == 0 else "ldk1")
                pb.mark(f"ld_kk{b_}_{r}", "ldk0" if b_ % 2 == 0 else "ldk1")
            w2 = [f"ag2_{b_}"] + (w + ["t_strip5"] if b_ <= 1 else [])
            for r in range(NCORE):
                for j in range(2):
                    pb.dma("gpsimd", lambda E, b_=b_, r=r, buf=buf, j=j: E.dma_start(
                        out=kv_v[buf][:, (2 * r + j) * 1040:(2 * r + j + 1) * 1040]
                        .rearrange("p (h e) -> p h e", h=16)[:, :, 0:64],
                        in_=agv[b_, r, j].rearrange("p (h e) -> p h e", h=16)),
                        waits=(w2 if r == 0 and j == 0 else ()),
                        sem="ldv0" if b_ % 2 == 0 else "ldv1")
                pb.mark(f"ld_kv{b_}_{r}", "ldv0" if b_ % 2 == 0 else "ldv1")

        kv_load(0)
        kv_load(1)

        NP = H // 2
        NGG = B * NP * NG

        def pair_tail_a(hhp):
            b2, hp = hhp // NP, hhp % NP
            bank = pC if hhp % 2 == 0 else pD
            GL = hhp * NG + NG - 1
            w = [f"t_aB{GL}"]
            if hhp >= 1:
                w.append(f"t_rb{hhp - 1}")
            def v_lp(E, bank=bank):
                return E.reciprocal(nrm[64:65, 0:512], bank[64:65, 0:512])
            pb.op("vector", v_lp, inc="cv", waits=w)
            pb.mark(f"v_lp{hhp}", "cv")

        def rest_tail(hhp):
            b2, hp = hhp // NP, hhp % NP
            bank = pC if hhp % 2 == 0 else pD
            pRB = pE if hhp % 2 == 0 else pF
            w = [f"v_lp{hhp}"]
            if hhp >= 2:
                w.append(f"s_rb{hhp - 2}")
            if hhp == 0:
                w.append("s_biasC")
            if hp == 1 and b2 >= 1:
                w.append(f"v_ye{b2 - 1}_3")
            pb.op("tensor", lambda E, pRB=pRB: E.matmul(
                pRB[0:64, 0:512], onescol[64:65, 0:64], nrm[64:65, 0:512],
                start=True, stop=True), inc="ct", waits=w)
            pb.mark(f"t_rb{hhp}", "ct")

            w = [f"t_rb{hhp}"]
            if hhp >= 1:
                w.append(f"v_noB{hhp - 1}")
            pb.op("vector", lambda E, pRB=pRB: E.tensor_copy(
                nrm[0:64, 0:512], pRB[0:64, 0:512]), inc="cv", waits=w)
            pb.mark(f"s_rb{hhp}", "cv")

            w = [f"s_rb{hhp}"]
            if hp == 0 and b2 >= 1:
                w.append(f"t_y{b2 - 1}_3")
                if d_otn is not None:
                    w.append(f"dbg_otn{b2 - 1}")
            pb.op("vector", lambda E, bank=bank, hp=hp: E.tensor_mul(
                otn128[0:64, hp * 256:hp * 256 + 256], bank[0:64, 0:256],
                nrm[0:64, 0:256]), inc="cv", waits=w)
            pb.mark(f"v_noA{hhp}", "cv")

            w = [f"s_rb{hhp}"]
            if hhp >= 2:
                w.append(f"ld_mv{hhp - 2}")
            pb.op("vector", lambda E, bank=bank, hp=hp: E.tensor_mul(
                otnstage[0:64, (hp % 2) * 256:(hp % 2) * 256 + 256],
                bank[0:64, 256:512], nrm[0:64, 256:512]), inc="cv", waits=w)
            pb.mark(f"v_noB{hhp}", "cv")

            w = [f"v_noB{hhp}"]
            if hp == 0 and b2 >= 1:
                w.append(f"t_y{b2 - 1}_3")
                if d_otn is not None:
                    w.append(f"dbg_otn{b2 - 1}")
            pb.dma("sync", lambda E, hp=hp: E.dma_start(
                out=otn128[64:128, hp * 256:hp * 256 + 256],
                in_=otnstage[0:64, (hp % 2) * 256:(hp % 2) * 256 + 256]),
                waits=w, sem="ldm0" if hp % 2 == 0 else "ldm1")
            pb.mark(f"ld_mv{hhp}", "ldm0" if hp % 2 == 0 else "ldm1")

        def emit_av(GG):
            hhp, g = GG // NG, GG % NG
            b2, hp = hhp // NP, hhp % NP
            buf2 = b2 % 2
            bank = pC if hhp % 2 == 0 else pD
            for side, hx in (("A", 2 * hp), ("B", 2 * hp + 1)):
                coff = 0 if side == "A" else 256
                w = [f"s_e{GG}", f"ld_kv{b2}_{NCORE - 1}"]
                if g == 0 and side == "A":
                    if hhp >= 2:
                        w.append(f"v_noB{hhp - 2}")
                eoff2 = (GG % 4) * 1024 + (0 if side == "A" else 512)
                def t_a(E, bank=bank, coff=coff, g=g, hx=hx, buf2=buf2, eoff2=eoff2, side=side):
                    r = None
                    for jj in range(GRP):
                        jt = g * GRP + jj
                        r = E.matmul(
                            bank[0:65, coff:coff + 256],
                            kv_v[buf2][:, jt * 1040 + hx * 65: jt * 1040 + hx * 65 + 65],
                            EbufM[:, eoff2 + jj * 256:eoff2 + jj * 256 + 256],
                            start=(side == "A" and g == 0 and jj == 0),
                            stop=(side == "B" and g == NG - 1 and jj == GRP - 1))
                    return r
                pb.op("tensor", t_a, inc="ct", waits=w)
                pb.mark(f"t_a{side}{GG}", "ct")

        def batch_tail(b_):
            for tt in range(2):
                for oc in range(2):
                    yc = tt * 2 + oc
                    w = []
                    if b_ >= 1 or yc >= 2:
                        pv = (b_, yc - 2) if yc >= 2 else (b_ - 1, yc + 2)
                        w.append(f"v_sx{pv[0]}_{pv[1]}")
                    pb.dma("gpsimd", lambda E, b_=b_, tt=tt, oc=oc, yc=yc: E.dma_start(
                        out=xch[:, (yc % 2) * 512:(yc % 2) * 512 + 512],
                        in_=x_in[b_, tt * 128:(tt + 1) * 128, oc * 512:(oc + 1) * 512]),
                        waits=w, sem="ldc0" if yc % 2 == 0 else "ldc1")
                    pb.mark(f"ld_xc{b_}_{yc}", "ldc0" if yc % 2 == 0 else "ldc1")

                    w = [f"v_noA{b_ * NP + NP - 1}", f"ld_mv{b_ * NP + NP - 1}",
                         f"ld_mv{b_ * NP + NP - 2}", "lw_wout", "lw_bo2", "g_vones"]
                    if b_ * 4 + yc >= 1:
                        pv = (b_, yc - 1) if yc >= 1 else (b_ - 1, 3)
                        w.append(f"v_ye{pv[0]}_{pv[1]}")
                    def t_y(E, tt=tt, oc=oc):
                        for ct in range(ND):
                            E.matmul(pF[:, :],
                                     otn128[:, ct * 256 + tt * 128: ct * 256 + tt * 128 + 128],
                                     w_out_sb[:, ct * D + oc * 512: ct * D + oc * 512 + 512],
                                     start=(ct == 0), stop=False)
                        if oc == 0:
                            return E.matmul(pF[:, :], bo2[0:1, 0:128],
                                            bo2[0:1, 128:640], start=False, stop=True)
                        return E.matmul(pF[:, :], bo2[64:65, 512:640],
                                        bo2[64:65, 0:512], start=False, stop=True)
                    pb.op("tensor", t_y, inc="ct", waits=w)
                    pb.mark(f"t_y{b_}_{yc}", "ct")

                    def v_ye(E, yc=yc):
                        return E.tensor_add(xrs[:, (yc % 2) * 512:(yc % 2) * 512 + 512],
                                            pF[:, :], xch[:, (yc % 2) * 512:(yc % 2) * 512 + 512])
                    w = [f"t_y{b_}_{yc}", f"ld_xc{b_}_{yc}"]
                    if b_ * 4 + yc >= 2:
                        pv = (b_, yc - 2) if yc >= 2 else (b_ - 1, yc + 2)
                        w.append(f"d_xr{pv[0]}_{pv[1]}")
                        if d_xres is not None:
                            w.append(f"dbg_xr{pv[0]}_{pv[1]}")
                    pb.op("vector", v_ye, inc="cv", waits=w)
                    pb.mark(f"v_ye{b_}_{yc}", "cv")

                    i2 = b_ * 2 + tt
                    def v_stat(E, oc=oc, yc=yc, i2=i2):
                        xc = xrs[:, (yc % 2) * 512:(yc % 2) * 512 + 512]
                        sq = xch[:, (yc % 2) * 512:(yc % 2) * 512 + 512]
                        sa = sa0 if oc == 0 else sa1
                        sb = sb0 if oc == 0 else sb1
                        E.reduce_sum(sa[:, i2:i2 + 1], xc, axis=AX)
                        E.tensor_mul(sq, xc, xc)
                        return E.reduce_sum(sb[:, i2:i2 + 1], sq, axis=AX)
                    pb.op("vector", v_stat, inc="cv", waits=[])
                    pb.mark(f"v_sx{b_}_{yc}", "cv")

                    pb.dma("sync", lambda E, b_=b_, tt=tt, oc=oc, yc=yc: E.dma_start(
                        out=scr_xres[b_, tt * 128:(tt + 1) * 128, oc * 512:(oc + 1) * 512],
                        in_=xrs[:, (yc % 2) * 512:(yc % 2) * 512 + 512]),
                        waits=[f"v_sx{b_}_{yc}"], sem="cdx0" if yc % 2 == 0 else "cdx1")
                    pb.mark(f"d_xr{b_}_{yc}", "cdx0" if yc % 2 == 0 else "cdx1")
                    if d_xres is not None:
                        pb.dma("sync", lambda E, b_=b_, tt=tt, oc=oc, yc=yc: E.dma_start(
                            out=d_xres[b_, tt * 128:(tt + 1) * 128, oc * 512:(oc + 1) * 512],
                            in_=xrs[:, (yc % 2) * 512:(yc % 2) * 512 + 512]),
                            waits=[f"v_sx{b_}_{yc}"], sem="dg0" if yc % 2 == 0 else "dg1")
                        pb.mark(f"dbg_xr{b_}_{yc}", "dg0" if yc % 2 == 0 else "dg1")

            if d_otn is not None:
                pb.dma("gpsimd", lambda E, b_=b_: E.dma_start(
                    out=d_otn[b_], in_=otn128[:, :]),
                    waits=[f"v_noA{b_ * NP + NP - 1}", f"ld_mv{b_ * NP + NP - 1}",
                           f"ld_mv{b_ * NP + NP - 2}"], sem="stg")
                pb.mark(f"dbg_otn{b_}", "stg")
            pb.mark(f"t_avd{b_}", "ct")
            if b_ + 2 < B:
                kv_load(b_ + 2)

        for GG in range(NGG + 8):
            if GG < NGG:
                hhp, g = GG // NG, GG % NG
                b_, hp = hhp // NP, hhp % NP
                buf = b_ % 2
                dt = hp
                bank_s = pA if GG % 2 == 0 else pB
                w = [f"ld_kk{b_}_{NCORE - 1}", "s_biasC"]
                if GG >= 2:
                    w.append(f"s_e{GG - 2}")
                def t_sp(E, g=g, dt=dt, b_=b_, buf=buf, bank_s=bank_s):
                    r = None
                    for jj in range(GRP):
                        jt = g * GRP + jj
                        for hof, soff in ((0, 0), (64, 512)):
                            r = E.matmul(
                                bank_s[:, soff + jj * 256:soff + jj * 256 + 256],
                                kv_k[buf][hof:hof + 64,
                                          dt * T + jt * 128: dt * T + jt * 128 + 128],
                                qT[hof:hof + 64,
                                   dt * T + b_ * PL: dt * T + b_ * PL + PL],
                                start=True, stop=True)
                    return r
                pb.op("tensor", t_sp, inc="ct", waits=w)
                pb.mark(f"t_sp{GG}", "ct")

            if 1 <= GG <= NGG:
                emit_av(GG - 1)
                if (GG - 1) % NG == NG - 1:
                    pair_tail_a((GG - 1) // NG)

            if GG >= 5 and (GG - 5) % NG == NG - 1:
                rest_tail((GG - 5) // NG)

            if GG >= 7 and (GG - 7) % NG == NG - 1:
                hhp_p = (GG - 7) // NG
                if hhp_p % NP == NP - 1:
                    batch_tail(hhp_p // NP)

            if GG < NGG:
                hhp, g = GG // NG, GG % NG
                hp = hhp % NP
                b_ = hhp // NP
                bank_s = pA if GG % 2 == 0 else pB
                eoff = (GG % 4) * 1024
                w = [f"t_sp{GG}"]
                if GG >= 4:
                    w += [f"t_aA{GG - 4}", f"t_aB{GG - 4}"]
                pb.op("scalar", lambda E, bank_s=bank_s, eoff=eoff, hp=hp, b_=b_: E.activation(
                    EbufM[:, eoff:eoff + 1024], bank_s[:, :], AF.Exp,
                    bias=biasC[:, hp * 8 + b_:hp * 8 + b_ + 1], scale=0.125),
                    inc="cs", waits=w)
                pb.mark(f"s_e{GG}", "cs")

        pb.mark("cdx0_all", "cdx0")
        pb.mark("cdx1_all", "cdx1")
        pb.mark("dg0_all", "dg0")
        pb.mark("dg1_all", "dg1")

        def v_fin2(E):
            E.tensor_add(sc1[:, :], sa0[:, :], sa1[:, :])
            E.tensor_add(sc2[:, :], sb0[:, :], sb1[:, :])
            E.tensor_scalar(st2m[:, :], sc1[:, :], 1.0 / D, None, ALU.mult)
            E.tensor_scalar(sa0[:, :], sc2[:, :], 1.0 / D, None, ALU.mult)
            E.tensor_mul(sa1[:, :], st2m[:, :], st2m[:, :])
            E.memset(vspc[:, :], 0.0)
            E.tensor_sub(sb0[:, :], sa0[:, :], sa1[:, :])
            E.memset(vspc[:, :], 1.0)
            return E.tensor_scalar(sb1[:, :], sb0[:, :], LN_EPS, None, ALU.add)
        pb.op("vector", v_fin2, inc="cv", waits=[f"v_sx{B - 1}_3"])
        pb.mark("v_fin2", "cv")
        def s_rstd2(E):
            E.activation(sc2[:, :], sb1[:, :], AF.Ln)
            E.memzero(sspc[:, :])
            return E.activation(st2r[:, :], sc2[:, :], AF.Exp, scale=-0.5)
        pb.op("scalar", s_rstd2, inc="cs", waits=["v_fin2"])
        pb.mark("s_rstd2", "cs")

        if phases < 5:
            pb.wait("gpsimd", [("sto", pb.counts["sto"]), ("stg", pb.counts["stg"]),
                               ("cs", pb.counts["cs"]),
                               ("cv", pb.counts["cv"]), ("cdx0", pb.counts["cdx0"]), ("cdx1", pb.counts["cdx1"]),
                               ("cdc", pb.counts["cdc"])])
            return

        for i in range(6):
            pb.dma("gpsimd", lambda E, i=i: E.dma_start(
                out=Wg[i][:, :].rearrange("p (a c) -> p a c", a=ND),
                in_=agw[i].rearrange("a p c -> p a c")),
                waits=(["agw_done", f"t_avd{B - 1}"] if i == 0 else ()), sem="lwg")
            pb.mark(f"lw_g{i}", "lwg")

        for b_ in range(B):
            for tt in range(2):
                i2 = b_ * 2 + tt
                w = ["cdx0_all", "cdx1_all"] + (["dg0_all", "dg1_all"] if debug else [])
                if i2 >= 2:
                    w.append(f"v_ap2{i2 - 2}")
                else:
                    w.append(f"v_ye{B - 1}_3")
                pb.dma("sync", lambda E, b_=b_, tt=tt: E.dma_start(
                    out=xrl[:, tt * D:(tt + 1) * D],
                    in_=scr_xres[b_, tt * 128:(tt + 1) * 128, :]), waits=w,
                    sem="ldr0" if i2 % 2 == 0 else "ldr1")
                pb.mark(f"ld_x2{i2}", "ldr0" if i2 % 2 == 0 else "ldr1")
                xn = xnst0 if i2 % 2 == 0 else xnst1
                w = [f"ld_x2{i2}", "s_rstd2"]
                if i2 >= 2:
                    w.append(f"t_x2t{i2 - 2}")
                pb.op("vector", lambda E, i2=i2, tt=tt, xn=xn: E.tensor_scalar(
                    xn[:, :], xrl[:, tt * D:(tt + 1) * D], st2m[:, i2:i2 + 1],
                    st2r[:, i2:i2 + 1], ALU.subtract, ALU.mult),
                    inc="cv", waits=w)
                pb.mark(f"v_ap2{i2}", "cv")
                w = [f"v_ap2{i2}"]
                if i2 >= 1:
                    w.append(f"s_t2e{i2 - 1}")
                else:
                    w.append(f"v_noB{B * (H // 2) - 1}")
                    w.append(f"v_ye{B - 1}_3")
                def t_x2t(E, xn=xn):
                    r = None
                    for dt in range(ND):
                        pt = pC if dt < 4 else pD
                        r = E.matmul(pt[:, (dt % 4) * 128:(dt % 4) * 128 + 128],
                                     xn[:, dt * 128:(dt + 1) * 128], ident[:, :],
                                     start=True, stop=True)
                    return r
                pb.op("tensor", t_x2t, inc="ct", waits=w)
                pb.mark(f"t_x2t{i2}", "ct")
                def s_t2e(E, i2=i2):
                    v = xnT_dst(xn2T, i2)
                    E.copy(v[:, 0:4, :], pC[:, :].rearrange("p (a c) -> p a c", a=4))
                    return E.copy(v[:, 4:8, :], pD[:, :].rearrange("p (a c) -> p a c", a=4))
                pb.op("scalar", s_t2e, inc="cs", waits=[f"t_x2t{i2}"])
                pb.mark(f"s_t2e{i2}", "cs")

        if d_xn2T is not None:
            pb.dma("gpsimd", lambda E: E.dma_start(out=d_xn2T, in_=xn2T[:, :]),
                   waits=[f"s_t2e{2 * B - 1}"], sem="stg")
        pb.mark("stg_all", "stg")
        if phases < 6:
            pb.wait("gpsimd", [("sto", pb.counts["sto"]), ("stg", pb.counts["stg"]),
                               ("cs", pb.counts["cs"]),
                               ("cv", pb.counts["cv"]), ("cdx0", pb.counts["cdx0"]), ("cdx1", pb.counts["cdx1"]),
                               ("cdc", pb.counts["cdc"])])
            return

        pb.dma("sync", lambda E: E.dma_start(
            out=bcol[:, :].rearrange("p (v a) -> p v a", v=4),
            in_=bvec_in.rearrange("v (a p) -> p v a", p=128),
            allow_slow_non_contiguous=True),
            waits=[f"t_avd{B - 1}"], sem="ldb")
        pb.dma("sync", lambda E: E.dma_start(
            out=inccol[:, :], in_=inc_in.rearrange("(a p) -> p a", p=128),
            allow_slow_non_contiguous=True), sem="ldb")
        pb.mark("ld_inc", "ldb")
        pb.marks["ld_bcol"] = pb.marks["ld_inc"]
        pb.op("gpsimd", lambda E: E.memset(ones2[:, :], 1.0), inc="cg",
              waits=[f"t_avd{B - 1}"])
        pb.mark("g_ones2", "cg")

        def v_c0(E):
            r = None
            for ct in range(ND):
                E.tensor_scalar(c_st[0][:, ct * PL:(ct + 1) * PL], ones2[:, :],
                                inccol[:, ct:ct + 1], None, ALU.mult)
                r = E.tensor_scalar(c_bf0[:, ct * PL:(ct + 1) * PL], ones2[:, :],
                                    inccol[:, ct:ct + 1], None, ALU.mult)
            return r
        pb.op("vector", v_c0, inc="cv",
              waits=["ld_inc", "g_ones2", f"v_ap2{2 * B - 1}", f"v_ye{B - 1}_3", "stg_all"])
        pb.mark("v_cup_-1_7", "cv")

        for t in range(B):
            cprev = c_st[t % 2]
            cnew = c_st[1 - t % 2]
            cbfprev = c_bf0 if t % 2 == 0 else c_bf1
            cbfnew = c_bf1 if t % 2 == 0 else c_bf0
            for ct in range(ND):
                if ct % 2 == 0:
                    gdst = [pA[:, 0:256], pA[:, 512:768], pB[:, 0:256]]
                else:
                    gdst = [pB[:, 512:768], pC[:, 0:256], pD[:, 0:256]]
                w = ["lw_g5", f"s_t2e{2 * t + 1}"]
                if t == 0 and ct == 0:
                    w.append(f"s_e{B * (H // 2) * NG - 1}")
                    w.append(f"s_e{B * (H // 2) * NG - 2}")
                if t == 0 and ct == 1:
                    w.append(f"s_t2e{2 * B - 1}")
                if t > 0 or ct > 1:
                    pv = (t, ct - 2) if ct >= 2 else (t - 1, ct + 6)
                    w.append(f"s_g{pv[0]}_{pv[1]}")
                if d_g00 is not None and t == 0 and ct == 2:
                    w.append("dbg_g00")
                def t_wx(E, gdst=gdst, t=t, ct=ct):
                    r = None
                    for gi in range(3):
                        dstp = gdst[gi]
                        for dt in range(ND):
                            r = E.matmul(dstp,
                                         Wg[gi][:, dt * D + ct * 128: dt * D + ct * 128 + 128],
                                         xn2T[:, dt * T + t * PL: dt * T + t * PL + PL],
                                         start=(dt == 0), stop=False)
                    return r
                pb.op("tensor", t_wx, inc="ct", waits=w)
                pb.mark(f"t_wx{t}_{ct}", "ct")

                w = [f"v_cup_{t - 1}_7", "lw_g5"]
                def t_uc(E, gdst=gdst, ct=ct, cbfprev=cbfprev):
                    r = None
                    for gi in range(3):
                        dstp = gdst[gi]
                        for dt in range(ND):
                            r = E.matmul(dstp,
                                         Wg[3 + gi][:, dt * D + ct * 128: dt * D + ct * 128 + 128],
                                         cbfprev[:, dt * PL:(dt + 1) * PL],
                                         start=False, stop=(dt == ND - 1))
                    return r
                pb.op("tensor", t_uc, inc="ct", waits=w)
                pb.mark(f"t_uc{t}_{ct}", "ct")

                def s_g(E, gdst=gdst, ct=ct):
                    E.activation(ftb[:, :], gdst[0], AF.Sigmoid,
                                 bias=bcol[:, 0 * 8 + ct:0 * 8 + ct + 1])
                    E.activation(itb[:, :], gdst[1], AF.Sigmoid,
                                 bias=bcol[:, 1 * 8 + ct:1 * 8 + ct + 1])
                    return E.activation(gtb[:, :], gdst[2], AF.Tanh,
                                        bias=bcol[:, 2 * 8 + ct:2 * 8 + ct + 1])
                w = [f"t_uc{t}_{ct}", "ld_bcol"]
                if t > 0 or ct > 0:
                    pv = (t, ct - 1) if ct >= 1 else (t - 1, 7)
                    w.append(f"v_cup_{pv[0]}_{pv[1]}")
                if d_g00 is not None and t == 0 and ct == 1:
                    w.append("dbg_g00")
                pb.op("scalar", s_g, inc="cs", waits=w)
                pb.mark(f"s_g{t}_{ct}", "cs")
                if d_g00 is not None and t == 0 and ct == 0:
                    pb.dma("sync", lambda E: E.dma_start(out=d_g00[0], in_=ftb[:, :]),
                           waits=["s_g0_0"], sem="sto")
                    pb.dma("sync", lambda E: E.dma_start(out=d_g00[1], in_=itb[:, :]),
                           waits=["s_g0_0"], sem="sto")
                    pb.dma("sync", lambda E: E.dma_start(out=d_g00[2], in_=gtb[:, :]),
                           waits=["s_g0_0"], sem="sto")
                    def s_pre0cp(E):
                        r = None
                        for gi in range(3):
                            r = E.copy(xrl[:, gi * PL:(gi + 1) * PL], gdst[gi])
                        return r
                    pb.op("scalar", s_pre0cp, inc="cs", waits=["s_g0_0"])
                    pb.mark("s_pre0cp", "cs")
                    pb.dma("sync", lambda E: E.dma_start(
                        out=d_pre0.rearrange("g p c -> p g c"),
                        in_=xrl[:, 0:3 * PL].rearrange("p (g c) -> p g c", g=3)),
                        waits=["s_pre0cp"], sem="sto")
                    pb.dma("sync", lambda E: E.dma_start(out=d_wg0, in_=Wg[0][:, :]),
                           waits=["lw_g5"], sem="sto")
                    pb.mark("dbg_g00", "sto")

                def v_cup(E, ct=ct, cprev=cprev, cnew=cnew, cbfnew=cbfnew):
                    sl = slice(ct * PL, (ct + 1) * PL)
                    E.tensor_mul(g1s[:, :], ftb[:, :], cprev[:, sl])
                    E.tensor_mul(g2s[:, :], itb[:, :], gtb[:, :])
                    E.memset(vspc[:, :], 0.0)
                    E.tensor_add(cnew[:, sl], g1s[:, :], g2s[:, :])
                    E.memset(vspc[:, :], 1.0)
                    return E.tensor_copy(cbfnew[:, sl], cnew[:, sl])
                w = [f"s_g{t}_{ct}"]
                if t == 0 and ct == 0 and d_otn is not None:
                    w.append("stg_all")
                if d_c1 is not None and t == 2 and ct == 0:
                    w.append("dbg_c1")
                pb.op("vector", v_cup, inc="cv", waits=w)
                pb.mark(f"v_cup_{t}_{ct}", "cv")

        if d_c1 is not None:
            pb.dma("sync", lambda E: E.dma_start(out=d_c1, in_=c_st[1][:, :]),
                   waits=["v_cup_0_7"], sem="sto")
            pb.mark("dbg_c1", "sto")

        cf_bf = c_bf1 if (B - 1) % 2 == 0 else c_bf0
        cf_st = c_st[1 - (B - 1) % 2]
        if d_cf is not None:
            pb.dma("sync", lambda E: E.dma_start(out=d_cf, in_=cf_st[:, :]),
                   waits=[f"v_cup_{B - 1}_7"], sem="sto")
        if phases < 7:
            pb.wait("gpsimd", [("sto", pb.counts["sto"]), ("stg", pb.counts["stg"]),
                               ("cs", pb.counts["cs"]),
                               ("cv", pb.counts["cv"]), ("cdx0", pb.counts["cdx0"]), ("cdx1", pb.counts["cdx1"]),
                               ("cdc", pb.counts["cdc"])])
            return

        for i, nm in enumerate(["uo", "wo"]):
            dst = uoT if nm == "uo" else woT
            pb.dma("gpsimd", lambda E, i=i, dst=dst: E.dma_start(
                out=dst[:, :].rearrange("p (a c) -> p a c", a=ND),
                in_=agw[7 - i].rearrange("a p c -> p a c")),
                waits=([f"t_uc{B - 1}_7"] if i == 0 else ()), sem="lwg")
            pb.mark(f"lw_{nm}", "lwg")

        pb.dma("sync", lambda E: E.dma_start(out=borow2[0:1, :], in_=bvec_in[3:4, :]),
               waits=[f"t_y{B - 1}_3"], sem="ldn")
        pb.mark("ld_bo", "ldn")
        for oc in range(2):
            w = ["ld_bo", f"s_g{B - 1}_7"] if oc == 0 else ["s_bo0"]
            pb.op("tensor", lambda E, oc=oc: E.matmul(
                pD[:, :], onescol[0:1, :], borow2[0:1, oc * 512:(oc + 1) * 512],
                start=True, stop=True), inc="ct", waits=w)
            pb.mark(f"t_bo{oc}", "ct")
            pb.op("scalar", lambda E, oc=oc: E.copy(bo_b[:, oc * 512:(oc + 1) * 512],
                                                    pD[:, :]), inc="cs", waits=[f"t_bo{oc}"])
            pb.mark(f"s_bo{oc}", "cs")

        for pt_ in range(2):
            for oc in range(2):
                ch = pt_ * 2 + oc
                w = ["lw_wo", f"v_cup_{B - 1}_7", "s_bo1"]
                if ch >= 1:
                    w.append(f"v_cuo{ch - 1}")
                def t_cuo(E, pt_=pt_, oc=oc, cf_bf=cf_bf):
                    r = None
                    for dt in range(ND):
                        r = E.matmul(pC[:, :],
                                     cf_bf[:, dt * PL + pt_ * 128: dt * PL + pt_ * 128 + 128],
                                     uoT[:, dt * D + oc * 512: dt * D + oc * 512 + 512],
                                     start=(dt == 0), stop=(dt == ND - 1))
                    return r
                pb.op("tensor", t_cuo, inc="ct", waits=w)
                pb.mark(f"t_cuo{ch}", "ct")
                pb.op("vector", lambda E, pt_=pt_, oc=oc: E.tensor_add(
                    cuo_bo[:, pt_ * D + oc * 512: pt_ * D + oc * 512 + 512],
                    pC[:, :], bo_b[:, oc * 512:(oc + 1) * 512]),
                    inc="cv", waits=[f"t_cuo{ch}", f"v_cup_{B - 1}_7"])
                pb.mark(f"v_cuo{ch}", "cv")

        nst = 0
        for t in range(B):
            for tt in range(2):
                w = []
                if t >= 1:
                    w.append(f"v_o{t - 1}_{tt * 2 + 1}")
                else:
                    w.append(f"v_ap2{2 * B - 1}")
                    if d_g00 is not None:
                        w.append("dbg_g00")
                pb.dma("sync", lambda E, t=t, tt=tt: E.dma_start(
                    out=xrl[:, tt * D:(tt + 1) * D],
                    in_=scr_xres[t, tt * 128:(tt + 1) * 128, :]), waits=w,
                    sem="ldr0" if tt == 0 else "ldr1")
                pb.mark(f"ld_x3{t}_{tt}", "ldr0" if tt == 0 else "ldr1")
            for pt_ in range(2):
                for oc in range(2):
                    ch = pt_ * 2 + oc
                    ph = pC if nst % 2 == 0 else pD
                    w = ["lw_wo", "v_cuo3"]
                    if nst >= 2:
                        w.append(f"s_h{nst - 2}")
                    def t_h(E, ph=ph, pt_=pt_, oc=oc, t=t):
                        r = None
                        for dt in range(ND):
                            r = E.matmul(ph[:, :],
                                         xn2T[:, dt * T + t * PL + pt_ * 128:
                                              dt * T + t * PL + pt_ * 128 + 128],
                                         woT[:, dt * D + oc * 512: dt * D + oc * 512 + 512],
                                         start=(dt == 0), stop=(dt == ND - 1))
                        return r
                    pb.op("tensor", t_h, inc="ct", waits=w)
                    pb.mark(f"t_h{nst}", "ct")

                    w = [f"t_h{nst}"]
                    if nst >= 2:
                        w.append(f"d_on{nst - 2}")
                    pb.op("vector", lambda E, ph=ph, pt_=pt_, oc=oc, nst=nst: E.tensor_add(
                        preslot[:, (nst % 2) * 512:(nst % 2) * 512 + 512], ph[:, :],
                        cuo_bo[:, pt_ * D + oc * 512: pt_ * D + oc * 512 + 512]),
                        inc="cv", waits=w)
                    pb.mark(f"v_h{nst}", "cv")

                    w = [f"v_h{nst}"]
                    if nst >= 2:
                        w.append(f"v_o{t if ch >= 2 else t - 1}_{(ch - 2) % 4}")
                    pb.op("scalar", lambda E, nst=nst: E.activation(
                        tanslot[:, (nst % 2) * 512:(nst % 2) * 512 + 512],
                        preslot[:, (nst % 2) * 512:(nst % 2) * 512 + 512], AF.Tanh),
                        inc="cs", waits=w)
                    pb.mark(f"s_h{nst}", "cs")

                    pb.op("vector", lambda E, nst=nst, pt_=pt_, oc=oc: E.tensor_add(
                        preslot[:, (nst % 2) * 512:(nst % 2) * 512 + 512],
                        tanslot[:, (nst % 2) * 512:(nst % 2) * 512 + 512],
                        xrl[:, pt_ * D + oc * 512: pt_ * D + oc * 512 + 512]),
                        inc="cv", waits=[f"s_h{nst}", f"ld_x3{t}_{pt_}"])
                    pb.mark(f"v_o{t}_{ch}", "cv")

                    pb.dma("sync", lambda E, t=t, pt_=pt_, oc=oc, nst=nst: E.dma_start(
                        out=out_t[t, pt_ * 128:(pt_ + 1) * 128, oc * 512:(oc + 1) * 512],
                        in_=preslot[:, (nst % 2) * 512:(nst % 2) * 512 + 512]),
                        waits=[f"v_o{t}_{ch}"], sem=f"so{nst % 2}")
                    pb.mark(f"d_on{nst}", f"so{nst % 2}")
                    nst += 1

        pb.wait("gpsimd", [("sto", pb.counts["sto"]), ("stg", pb.counts["stg"]),
                           ("so0", pb.counts["so0"]), ("so1", pb.counts["so1"]),
                           ("cdx0", pb.counts["cdx0"]), ("cdx1", pb.counts["cdx1"]),
                           ("cs", pb.counts["cs"]), ("cv", pb.counts["cv"])])

    with es:
        with nc.Block() as block:
            for eng in ENGINES:
                def run(E, eng=eng):
                    pb.begin(eng, E)
                    prog(pb)
                getattr(block, eng)(run)

    return nc, dbg


def make_inputs(ins, core):
    x = np.ascontiguousarray(ins["L0_attn_x"][:, 256 * core:256 * (core + 1), :])
    trow = np.tile(np.arange(256, dtype=np.float32) + 256 * core, B)[None, :]
    wffn_names = ["L0_ffn_wf", "L0_ffn_wi", "L0_ffn_wc", "L0_ffn_uf",
                  "L0_ffn_ui", "L0_ffn_uc", "L0_ffn_wo", "L0_ffn_uo"]
    bvec = np.stack([ins["L0_ffn_bf"], ins["L0_ffn_bi"], ins["L0_ffn_bc"],
                     ins["L0_ffn_bo"]], axis=0)
    return {
        "x": x,
        "wqkv": np.asarray(ins["L0_attn_w_qkv"]),
        "wout": np.asarray(ins["L0_attn_w_out"]),
        "bout": np.asarray(ins["L0_attn_b_out"])[None, :],
        "wffn": np.asarray(ins[wffn_names[core]]),
        "bvec": np.ascontiguousarray(bvec),
        "incv": np.asarray(ins["L0_ffn_in_c"]),
        "trow": np.ascontiguousarray(trow),
    }


def kernel(**inputs):
    nc, _ = build(debug=0, phases=99)
    in_maps = [make_inputs(inputs, c) for c in range(NCORE)]
    res = run_bass_kernel_spmd(nc, in_maps, core_ids=list(range(NCORE)))
    return np.concatenate([r["out"] for r in res.results], axis=1)



